# version 15
# baseline (speedup 1.0000x reference)
import numpy as np
import ml_dtypes
import concourse.bass as bass
import concourse.mybir as mybir
from concourse.bass_utils import run_bass_kernel_spmd

F32 = mybir.dt.float32
BF16 = mybir.dt.bfloat16
AF = mybir.ActivationFunctionType
ALU = mybir.AluOpType
AX = mybir.AxisListType

D = 2048
S = 4096
NQ = 2048
H = 16
HD = 128
NE = 8
RK = 64
BETA = 9.5
EPS = 1e-5

ENGS = ("tensor", "scalar", "vector", "sync", "gpsimd")


class Sem:
    def __init__(self, h, name):
        self.h = h
        self.name = name
        self.n = 0


class Rec:
    def __init__(self):
        self.ops = {e: [] for e in ENGS}
        self.waits = []

    def wait(self, e, sem, val):
        if val <= 0:
            return
        self.waits.append((sem, val))
        self.ops[e].append(lambda eng, s=sem, v=val: eng.wait_ge(s.h, v))

    def op(self, e, build, incs=()):
        incs = tuple(incs)
        for s, a in incs:
            s.n += a
        assert len(incs) <= 1, "one sem update per instruction"
        attach, spill = incs[:1], incs[1:]

        def f(eng, build=build, attach=attach, spill=spill):
            inst = build(eng)
            for s, a in attach:
                inst = inst.then_inc(s.h, a)
            for s, a in spill:
                eng.sem_inc(s.h, a)
        self.ops[e].append(f)

    def check(self):
        for s, v in self.waits:
            assert v <= s.n, f"wait on {s.name} for {v} but total incs {s.n}"


def dap(t, off, dims):
    return bass.AP(t, off, [list(d) for d in dims])


def build_nc():
    nc = bass.Bass("TRN2", target_bir_lowering=False)

    xTb = nc.dram_tensor("xTb", [D, S], F32, kind="ExternalInput")
    xnat = nc.dram_tensor("xnat", [NQ, D], F32, kind="ExternalInput")
    wqT = nc.dram_tensor("wqT", [D, D], F32, kind="ExternalInput")
    wkT = nc.dram_tensor("wkT", [D, D], F32, kind="ExternalInput")
    wvT = nc.dram_tensor("wvT", [D, D], F32, kind="ExternalInput")
    woT = nc.dram_tensor("woT", [D, D], F32, kind="ExternalInput")
    ATd = nc.dram_tensor("ATd", [D, NE * RK], F32, kind="ExternalInput")
    BTd = nc.dram_tensor("BTd", [4 * 128, D], F32, kind="ExternalInput")
    WrTd = nc.dram_tensor("WrTd", [D, NE], F32, kind="ExternalInput")
    brb = nc.dram_tensor("brb", [128, NE], F32, kind="ExternalInput")
    identd = nc.dram_tensor("identd", [128, 128], BF16, kind="ExternalInput")
    onesmd = nc.dram_tensor("onesmd", [128, 128], F32, kind="ExternalInput")
    onesmbd = nc.dram_tensor("onesmbd", [128, 128], BF16, kind="ExternalInput")
    onesrd = nc.dram_tensor("onesrd", [1, 128], BF16, kind="ExternalInput")

    y = nc.dram_tensor("y", [NQ, D], F32, kind="ExternalOutput")

    kdram = nc.dram_tensor("kdram", [8, 128, 8192], BF16)
    vdram = nc.dram_tensor("vdram", [S, D], BF16)
    qdram = nc.dram_tensor("qdram", [4, 128, 8192], BF16)
    negdram = nc.dram_tensor("negdram", [16, 2048], BF16)

    from contextlib import ExitStack
    ctx = ExitStack()
    e = ctx.enter_context

    AB = [e(nc.sbuf_tensor(f"AB{i}", [128, 8192], BF16)) for i in range(4)]
    A_ = [AB[i // 4][:, (i % 4) * 2048:(i % 4 + 1) * 2048] for i in range(16)]
    B_ = [e(nc.sbuf_tensor(f"B{i}", [128, 8192], BF16)) for i in range(4)]
    C_ = [e(nc.sbuf_tensor(f"C{i}", [128, 8192], BF16)) for i in range(3)]
    obuf = [e(nc.sbuf_tensor(f"obuf{i}", [128, 2048], F32)) for i in range(2)]
    mh = e(nc.sbuf_tensor("mh", [128, 1024], BF16))
    coefT = e(nc.sbuf_tensor("coefT", [128, 128], F32))
    stcol = e(nc.sbuf_tensor("stcol", [128, 64], F32))
    smf = e(nc.sbuf_tensor("smf", [128, 32], F32))
    lgts = e(nc.sbuf_tensor("lgts", [128, 16], F32))
    wrtb = e(nc.sbuf_tensor("wrtb", [128, 128], BF16))
    brs = e(nc.sbuf_tensor("brs", [128, NE], F32))
    identb = e(nc.sbuf_tensor("identb", [128, 128], BF16))
    onesm = e(nc.sbuf_tensor("onesm", [128, 128], F32))
    onesr = e(nc.sbuf_tensor("onesr", [1, 128], BF16))
    onesmb = e(nc.sbuf_tensor("onesmb", [128, 128], BF16))
    acc2 = e(nc.sbuf_tensor("acc2", [128, 2048], BF16))
    pa = e(nc.sbuf_tensor("pa", [128, 32], F32))
    pab = e(nc.sbuf_tensor("pab", [128, 16], BF16))
    X = e(nc.sbuf_tensor("X", [128, 2048], BF16))

    psA = e(nc.psum_tensor("psA", [128, 1024], F32))
    psB_cm = nc.psum_tensor("psB", [128, 1024], F32)
    psB = psB_cm.__enter__()
    psC_cm = nc.psum_tensor("psC", [128, 1024], F32)
    psC = psC_cm.__enter__()
    psF_cm = nc.psum_tensor("psF", [128, 1024], F32)
    psF = psF_cm.__enter__()

    semnames = ("wA wAf x0 x1 xf0 xf1 mmP1 evP1 sp kv0 kv1 kvg0 kvg1 sc exp avf "
                "bc nrm c0 c1 c2 op r1 st1 sq1 vch sch n1 tx "
                "asc ard stq xrg xr csd dnb rcp lnr xg0 xg1 xg2 xg3 xg4 xg5 xg6 xg7 "
                + "".join(f"c{s}p{p} " for s in range(3) for p in range(4))
                + "txev txes wF h lg co mk yy x2 n2 out ld0 ld0g").split()
    S_ = {n: Sem(e(nc.semaphore(f"s_{n}")), n) for n in semnames}

    r = Rec()

    for dst, src in ((identb, identd), (onesm, onesmd), (onesmb, onesmbd), (onesr, onesrd), (brs, brb)):
        r.wait("sync", S_["ld0"], S_["ld0"].n)
        r.op("sync", lambda eng, d=dst, s=src: eng.dma_start(d[:, :], s.ap()[:, :]),
             [(S_["ld0"], 16)])
    r.op("gpsimd", lambda eng: eng.dma_start(
        wrtb[:, :], dap(WrTd, 0, [[NE, 128], [128 * NE, 16], [1, NE]])), [(S_["ld0g"], 16)])
    lds0 = S_["ld0"].n

    def pool6(i):
        i = i % 6
        if i < 2:
            return psB[:, i * 512:(i + 1) * 512]
        if i < 4:
            return psC[:, (i - 2) * 512:(i - 2 + 1) * 512]
        return psF[:, (i - 4) * 512:(i - 4 + 1) * 512]

    g_grp = 0
    n_spills = 0
    vslot_i = 0
    stage_last_spill = {}
    xcnk_free_at = {}
    pass_end_mm = []

    passes = (("k", wkT, 8, kdram), ("v", wvT, 8, vdram), ("q", wqT, 4, qdram))
    x_after = {}

    def emit_xchunk(t):
        par = t % 2
        if par in xcnk_free_at:
            r.wait("gpsimd", S_["mmP1"], xcnk_free_at[par])
        r.op("gpsimd", lambda eng, par=par, t=t: eng.dma_start(
            C_[par][:, :], dap(xTb, t * 512, [[S, 128], [128 * S, 16], [1, 512]])),
            [(S_["x" + str(par)], 16)])
        x_after[t] = S_["x" + str(par)].n

    for p_i, (kind, wT, nchunks, dest) in enumerate(passes):
        emit_xchunk(0)
        w_after = {}
        for g4 in range(4):
            wsem = S_["wA"] if g4 % 2 == 0 else S_["wAf"]
            if p_i > 0:
                r.wait("gpsimd", S_["mmP1"],
                       pass_end_mm[p_i - 1] - 16 + 4 * (g4 + 1))
            r.wait("gpsimd", wsem, wsem.n)
            r.op("gpsimd", lambda eng, g4=g4, wT=wT: eng.dma_start(
                AB[g4][:, :],
                dap(wT, g4 * 512, [[D, 128], [128 * D, 16], [1, 512]])),
                [(wsem, 16)])
            w_after[g4] = (wsem, wsem.n)

        for t in range(nchunks):
            par = t % 2
            if t + 1 < nchunks:
                emit_xchunk(t + 1)
            r.wait("tensor", S_["x" + str(par)], x_after[t])

            if kind in ("k", "q"):
                for dk in range(16):
                    if dk % 4 == 0:
                        r.wait("tensor", *w_after[dk // 4])
                    slot = g_grp % 6
                    r.wait("tensor", S_["evP1"], g_grp - 5)
                    for mc in range(16):
                        incs = [(S_["mmP1"], 1)] if mc == 15 else []
                        r.op("tensor", lambda eng, slot=slot, mc=mc, dk=dk, par=par:
                             eng.matmul(pool6(slot),
                                        AB[dk // 4][:, mc * 512 + (dk % 4) * 128:
                                                    mc * 512 + (dk % 4) * 128 + 128],
                                        C_[par][:, mc * 512:(mc + 1) * 512],
                                        start=(mc == 0), stop=(mc == 15)), incs)
                    r.wait("scalar", S_["mmP1"], S_["mmP1"].n)
                    if dk == 0 and ("kq", par) in stage_last_spill:
                        r.wait("scalar", S_["sp"], 16 * stage_last_spill[("kq", par)])
                    r.op("scalar", lambda eng, slot=slot, dk=dk, par=par:
                         eng.copy(B_[par][:, dk * 512:(dk + 1) * 512], pool6(slot)),
                         [(S_["evP1"], 1)])
                    g_grp += 1
                r.wait("sync", S_["evP1"], S_["evP1"].n)
                r.wait("sync", S_["sp"], S_["sp"].n)
                r.op("sync", lambda eng, t=t, par=par, dest=dest:
                     eng.dma_start(dest.ap()[t, :, :], B_[par][:, :]), [(S_["sp"], 16)])
                n_spills += 1
                stage_last_spill[("kq", par)] = n_spills
            else:
                for dvc in range(4):
                    r.wait("tensor", *w_after[dvc])
                    for tt in range(4):
                        vs = (t % 2) * 4 + tt
                        tt_global = t * 4 + tt
                        slot = g_grp % 6
                        r.wait("tensor", S_["evP1"], g_grp - 5)
                        for mc in range(16):
                            incs = [(S_["mmP1"], 1)] if mc == 15 else []
                            r.op("tensor", lambda eng, slot=slot, mc=mc, tt=tt,
                                 dvc=dvc, par=par:
                                 eng.matmul(pool6(slot),
                                            C_[par][:, mc * 512 + tt * 128:
                                                    mc * 512 + (tt + 1) * 128],
                                            AB[dvc][:, mc * 512:(mc + 1) * 512],
                                            start=(mc == 0), stop=(mc == 15)), incs)
                        r.wait("scalar", S_["mmP1"], S_["mmP1"].n)
                        if dvc == 0 and ("v", vs) in stage_last_spill:
                            r.wait("scalar", S_["sp"], 16 * stage_last_spill[("v", vs)])
                        r.op("scalar", lambda eng, slot=slot, vs=vs, dvc=dvc:
                             eng.copy(B_[2 + vs // 4][:, (vs % 4) * 2048 + dvc * 512:
                                                      (vs % 4) * 2048 + (dvc + 1) * 512],
                                      pool6(slot)), [(S_["evP1"], 1)])
                        g_grp += 1
                        if dvc == 3:
                            r.wait("sync", S_["evP1"], S_["evP1"].n)
                            r.wait("sync", S_["sp"], S_["sp"].n)
                            r.op("sync", lambda eng, tt_global=tt_global, vs=vs:
                                 eng.dma_start(
                                     vdram.ap()[tt_global * 128:(tt_global + 1) * 128, :],
                                     B_[2 + vs // 4][:, (vs % 4) * 2048:(vs % 4 + 1) * 2048]),
                                 [(S_["sp"], 16)])
                            n_spills += 1
                            stage_last_spill[("v", vs)] = n_spills
            xcnk_free_at[par] = S_["mmP1"].n
        pass_end_mm.append(S_["mmP1"].n)

    P1_SPILLS = S_["sp"].n
    P1_MM = S_["mmP1"].n
    EV_TOT = S_["evP1"].n

    def ktbuf(par, piece):
        return A_[2 * par + piece]

    def vbuf(par, piece):
        return A_[4 + 2 * par + piece]

    def qbuf(par):
        return A_[8 + par]

    def ring(i):
        return A_[10 + (i % 10) // 2][:, ((i % 10) % 2) * 1024:(((i % 10) % 2) + 1) * 1024]

    def attnT(vh, lo, width):
        hh, qh = vh // 2, vh % 2
        base = (hh % 4) * 2048 + qh * 1024 + lo
        return B_[hh // 4][:, base:base + width]

    def chslot(vh):
        return acc2[:, (vh % 2) * 1024:(vh % 2 + 1) * 1024]

    def pslot(vh):
        return acc2[:, ((vh + 1) % 2) * 1024:((vh + 1) % 2 + 1) * 1024]

    NVH = 32
    kv_snap = {}
    kvv_snap = {}

    def emit_loads(hh):
        par = hh % 2
        if hh < 2:
            r.wait("sync", S_["sp"], P1_SPILLS)
            r.wait("sync", S_["mmP1"], P1_MM)
        else:
            r.wait("sync", S_["sc"], 64 * (hh - 1))
            r.wait("sync", S_["avf"], 64 * (hh - 1))
        ks = S_["kv" + str(par)]
        for piece in range(2):
            r.wait("sync", ks, ks.n)
            r.op("sync", lambda eng, hh=hh, g=piece, par=par: eng.dma_start(
                ktbuf(par, g)[:, :],
                dap(kdram, (4 * g) * 128 * 8192 + hh * 512,
                    [[8192, 128], [128 * 8192, 4], [1, 512]])), [(ks, 16)])
        r.wait("sync", ks, ks.n)
        r.op("sync", lambda eng, hh=hh, par=par: eng.dma_start(
            qbuf(par)[:, :],
            dap(qdram, hh * 512, [[8192, 128], [128 * 8192, 4], [1, 512]])),
            [(ks, 16)])
        kv_snap[hh] = ks.n
        for piece in range(2):
            r.wait("sync", ks, ks.n)
            r.op("sync", lambda eng, hh=hh, g=piece, par=par: eng.dma_start(
                vbuf(par, g)[:, :],
                dap(vdram, (g * 16 * 128) * D + hh * 128,
                    [[D, 128], [128 * D, 16], [1, 128]])), [(ks, 16)])
        kvv_snap[hh] = ks.n

    turns = []
    turn_of_kt = {}
    turn_of_den = {}
    for vh in range(NVH):
        for kt in range(32):
            if kt == 3 and vh >= 1:
                turn_of_den[vh - 1] = len(turns)
                turns.append(("den", vh - 1))
            turn_of_kt[(vh, kt)] = len(turns)
            turns.append(("kt", vh, kt))
    turn_of_den[31] = len(turns)
    turns.append(("den", 31))

    def turn_slot(T):
        return psA if T % 2 == 0 else psC

    def turn_consumed(T):
        t = turns[T]
        if t[0] == "kt":
            return (S_["exp"], t[1] * 32 + t[2] + 1)
        return (S_["lnr"], t[1] + 1)

    def emit_passA_mm(ga):
        vh, a = ga // 64, ga % 64
        hh, qh, par = vh // 2, vh % 2, (vh // 2) % 2
        j, c = a // 8, a % 8
        if a == 0:
            r.wait("tensor", S_["kv" + str(par)], kv_snap[hh])
        if ga == 0:
            r.wait("tensor", S_["evP1"], EV_TOT)
        r.wait("tensor", S_["ard"], ga - 1)
        r.op("tensor", lambda eng, ga=ga, qh=qh, par=par, j=j, c=c: eng.matmul(
            psF[:, (ga % 2) * 512:(ga % 2 + 1) * 512],
            qbuf(par)[:, qh * 1024 + j * 128:qh * 1024 + (j + 1) * 128],
            ktbuf(par, c // 4)[:, (c % 4) * 512:(c % 4 + 1) * 512],
            start=True, stop=True), [(S_["asc"], 1)])

    def emit_passA_dve(ga):
        vh, a = ga // 64, ga % 64
        j, c = a // 8, a % 8
        scc = (j % 2) * 8 + c
        r.wait("vector", S_["asc"], ga + 1)
        r.op("vector", lambda eng, ga=ga, scc=scc: eng.reduce_max(
            pa[:, scc:scc + 1], psF[:, (ga % 2) * 512:(ga % 2 + 1) * 512],
            axis=AX.X), [(S_["ard"], 1)])
        if c == 7:
            stc = (vh % 2) * 8 + j
            r.wait("vector", S_["ard"], S_["ard"].n)
            if vh >= 2:
                r.wait("vector", S_["xg" + str(j)], 16 * (vh - 1))
            r.op("vector", lambda eng, j=j, stc=stc: eng.reduce_max(
                pab[:, stc:stc + 1], pa[:, (j % 2) * 8:(j % 2) * 8 + 8],
                axis=AX.X, negate=True), [(S_["stq"], 1)])

    def emit_stat_dma(vh, j):
        stc = (vh % 2) * 8 + j
        qh = vh % 2
        xg = S_["xg" + str(j)]
        if vh == 0 and j == 0:
            r.wait("gpsimd", S_["sp"], P1_SPILLS)
        r.wait("gpsimd", xg, 16 * vh)
        if vh >= 2 and j == 0:
            r.wait("gpsimd", S_["sc"], 32 * (vh - 1))
        r.wait("gpsimd", S_["stq"], 8 * vh + j + 1)
        r.op("gpsimd", lambda eng, j=j, qh=qh, stc=stc: eng.dma_start(
            X[32 * qh:32 * qh + 1, qh * 1024 + j * 128:qh * 1024 + (j + 1) * 128],
            pab[:, stc:stc + 1]), [(xg, 16)])

    def casc_ops(vh, t):
        CH, P = (lambda: chslot(vh)), (lambda: pslot(vh))
        R = lambda k: (lambda: ring(32 * vh + k))
        if t == 5:
            return [(CH, R(0), R(1), [0, 1])]
        if 6 <= t <= 11:
            return [(CH, CH, R(t - 4), [t - 4])]
        if t == 12:
            return [(P, R(8), R(9), [8, 9])]
        if 13 <= t <= 18:
            return [(P, P, R(t - 3), [t - 3])]
        if t == 19:
            return [(P, P, CH, [])]
        if t == 20:
            return [(CH, R(16), R(17), [16, 17])]
        if 21 <= t <= 26:
            return [(CH, CH, R(t - 3), [t - 3])]
        if t == 27:
            return [(P, P, CH, [])]
        if t == 28:
            return [(CH, R(24), R(25), [24, 25])]
        if 29 <= t <= 33:
            return [(CH, CH, R(t - 3), [t - 3])]
        if t == 34:
            return [(CH, CH, R(31), [31])]
        if t == 35:
            return [(P, P, CH, [])]
        return []

    csd_at = {}
    csd_final = {}

    def emit_cascade(vh, t):
        for dst, s0, s1, cons in casc_ops(vh, t):
            if cons:
                r.wait("vector", S_["exp"], 32 * vh + max(cons) + 1)
            if t == 5:
                r.wait("vector", S_["dnb"], 2 * vh)
            r.wait("vector", S_["csd"], S_["csd"].n)
            r.op("vector", lambda eng, dst=dst, s0=s0, s1=s1: eng.tensor_add(
                dst(), s0(), s1()), [(S_["csd"], 1)])
            for k in cons:
                csd_at[32 * vh + k] = S_["csd"].n
            if t == 35:
                csd_final[vh] = S_["csd"].n

    def emit_den(vhd):
        T = turn_of_den[vhd]
        slot = turn_slot(T)
        if T >= 2:
            sem, v = turn_consumed(T - 2)
            r.wait("tensor", sem, v)
        r.wait("tensor", S_["exp"], 32 * (vhd + 1))
        r.wait("tensor", S_["csd"], csd_final[vhd])
        for qc in range(2):
            r.op("tensor", lambda eng, qc=qc, vhd=vhd, slot=slot: eng.matmul(
                slot[:, qc * 512:(qc + 1) * 512], onesmb[:, :],
                pslot(vhd)[:, qc * 512:(qc + 1) * 512],
                start=True, stop=True), [(S_["dnb"], 1)])

    def emit_recip(vhd):
        slot = turn_slot(turn_of_den[vhd])
        r.wait("scalar", S_["dnb"], 2 * (vhd + 1))
        r.wait("scalar", S_["rcp"], vhd - 1)
        r.op("scalar", lambda eng, vhd=vhd, slot=slot: eng.activation(
            obuf[0][:, (vhd % 2) * 1024:(vhd % 2 + 1) * 1024],
            slot[:, 0:1024], AF.Ln), [(S_["lnr"], 1)])
        r.wait("scalar", S_["lnr"], S_["lnr"].n)
        r.op("scalar", lambda eng, vhd=vhd: eng.activation(
            obuf[1][:, (vhd % 2) * 1024:(vhd % 2 + 1) * 1024],
            obuf[0][:, (vhd % 2) * 1024:(vhd % 2 + 1) * 1024],
            AF.Exp, scale=np.float32(-1.0)), [(S_["rcp"], 1)])

    def emit_evac(vhd, h):
        r.wait("vector", S_["rcp"], vhd + 1)
        r.wait("vector", S_["avf"], 32 * (vhd + 1))
        if vhd == 0 and h == 0:
            r.wait("vector", S_["sp"], P1_SPILLS)
        r.op("vector", lambda eng, h=h, vhd=vhd: eng.tensor_mul(
            attnT(vhd, h * 512, 512), psB[:, h * 512:(h + 1) * 512],
            obuf[1][:, (vhd % 2) * 1024 + h * 512:(vhd % 2) * 1024 + (h + 1) * 512]),
            [(S_["nrm"], 1)])

    def emit_expB(vh, kt):
        gk = 32 * vh + kt
        T = turn_of_kt[(vh, kt)]
        slot = turn_slot(T)
        r.wait("scalar", S_["sc"], gk + 1)
        if gk >= 10:
            r.wait("scalar", S_["avf"], gk - 9)
            r.wait("scalar", S_["csd"], csd_at[gk - 10])
        r.op("scalar", lambda eng, gk=gk, slot=slot: eng.activation(
            ring(gk), slot[:, :], AF.Exp), [(S_["exp"], 1)])

    def emit_scoresB(vh, kt):
        gk = 32 * vh + kt
        T = turn_of_kt[(vh, kt)]
        slot = turn_slot(T)
        qh, par = vh % 2, (vh // 2) % 2
        if kt == 0:
            for j in range(8):
                r.wait("tensor", S_["xg" + str(j)], 16 * (vh + 1))
        if T >= 2:
            sem, v = turn_consumed(T - 2)
            r.wait("tensor", sem, v)
        piece, tile = kt // 16, kt % 16
        for qc in range(2):
            r.op("tensor", lambda eng, qc=qc, qh=qh, slot=slot: eng.matmul(
                slot[:, qc * 512:(qc + 1) * 512], onesmb[32 * qh:32 * qh + 1, :],
                X[32 * qh:32 * qh + 1, qh * 1024 + qc * 512:qh * 1024 + (qc + 1) * 512],
                start=True, stop=False), [])
        for qc in range(2):
            r.op("tensor", lambda eng, piece=piece, tile=tile, qc=qc, par=par, qh=qh,
                 slot=slot:
                 eng.matmul(slot[:, qc * 512:(qc + 1) * 512],
                            ktbuf(par, piece)[:, tile * 128:(tile + 1) * 128],
                            qbuf(par)[:, qh * 1024 + qc * 512:qh * 1024 + (qc + 1) * 512],
                            start=False, stop=True),
                 [(S_["sc"], 1)] if qc == 1 else [])

    av_q = []
    for vhV in range(NVH):
        for ktV in range(32):
            av_q.append((vhV * 32 + max(9, ktV + 2), vhV, ktV))
    av_q.sort()

    def emit_av_pops(p, limit=2):
        n_av = 0
        while av_q and av_q[0][0] <= p and n_av < limit:
            _, vhV, ktV = av_q.pop(0)
            parV = (vhV // 2) % 2
            gkV = vhV * 32 + ktV
            if ktV == 0:
                r.wait("tensor", S_["kv" + str(parV)], kvv_snap[vhV // 2])
            r.wait("tensor", S_["exp"], gkV + 1)
            piece, tile = ktV // 16, ktV % 16
            for qc in range(2):
                if ktV == 0:
                    r.wait("tensor", S_["nrm"], 2 * (vhV - 1) + qc + 1)
                r.op("tensor", lambda eng, piece=piece, tile=tile, qc=qc, parV=parV,
                     gkV=gkV, ktV=ktV:
                     eng.matmul(psB[:, qc * 512:(qc + 1) * 512],
                                vbuf(parV, piece)[:, tile * 128:(tile + 1) * 128],
                                ring(gkV)[:, qc * 512:(qc + 1) * 512],
                                start=(ktV == 0), stop=(ktV == 31)),
                     [(S_["avf"], 1)] if qc == 1 else [])
            n_av += 1

    emit_loads(0)
    emit_loads(1)
    r.wait("tensor", S_["ld0"], lds0)
    for ga in range(100):
        emit_passA_mm(ga)
        emit_passA_dve(ga)
        if ga == 63:
            for j in range(8):
                emit_stat_dma(0, j)
    for j in range(4):
        emit_stat_dma(1, j)

    NPOS = NVH * 32
    for p in range(NPOS):
        vh, kt = p // 32, p % 32

        for hh in range(2, 16):
            if p == 64 * (hh - 1):
                emit_loads(hh)
        for j in range(4, 8):
            if p == max(0, 4 * (j - 4) + 2):
                emit_stat_dma(1, j)
        for vhg in range(2, NVH):
            for j in range(8):
                if p == 32 * (vhg - 2) + 18 + 4 * j:
                    emit_stat_dma(vhg, j)

        if kt == 3 and vh >= 1:
            emit_den(vh - 1)
        emit_scoresB(vh, kt)
        ga0 = 2 * (p + 50)
        if ga0 + 1 < 2048:
            emit_passA_mm(ga0)
            emit_passA_mm(ga0 + 1)
        emit_av_pops(p)

        if kt == 4 and vh >= 1:
            emit_recip(vh - 1)
        emit_expB(vh, kt)

        if kt == 0 and vh >= 1:
            emit_cascade(vh - 1, 32)
            emit_cascade(vh - 1, 33)
        if kt == 1 and vh >= 1:
            emit_cascade(vh - 1, 34)
        if kt == 2 and vh >= 1:
            emit_cascade(vh - 1, 35)
        if kt == 5 and vh >= 1:
            emit_evac(vh - 1, 0)
        if kt == 6 and vh >= 1:
            emit_evac(vh - 1, 1)
        if ga0 + 1 < 2048:
            emit_passA_dve(ga0)
            emit_passA_dve(ga0 + 1)
        emit_cascade(vh, kt)

    emit_av_pops(10 ** 9, limit=64)
    for t in (32, 33, 34, 35):
        emit_cascade(31, t)
    emit_den(31)
    emit_recip(31)
    emit_evac(31, 0)
    emit_evac(31, 1)
    NRM_TOTAL = S_["nrm"].n

    psF_cm.__exit__(None, None, None)
    psC_cm.__exit__(None, None, None)
    psB_cm.__exit__(None, None, None)
    psD = e(nc.psum_tensor("psD", [128, 1024], F32))
    psT = e(nc.psum_tensor("psT", [128, 2048], BF16))
    psE = e(nc.psum_tensor("psE", [128, 1024], F32))

    slot_items = []
    for g in range(4):
        slot_items.append(("wo", g))
        slot_items.append(("xr", g))
    slot_of = {}
    slot_uses = [0, 0, 0]
    for i, item in enumerate(slot_items):
        slot_of[item] = i % 3

    def item_consumed(item):
        knd, g = item
        return (S_["op"], 16 * (g + 1)) if knd == "wo" else (S_["r1"], 16 * (g + 1))

    c_piece = {}
    for i, (knd, g) in enumerate(slot_items):
        slot = i % 3
        if slot_uses[slot] == 0:
            r.wait("gpsimd", S_["mmP1"], P1_MM)
        else:
            sem, v = item_consumed(slot_items[i - 3])
            r.wait("gpsimd", sem, v)
        src = woT if knd == "wo" else xnat
        for pc in range(4):
            ps = S_[f"c{slot}p{pc}"]
            r.op("gpsimd", lambda eng, slot=slot, g=g, src=src, pc=pc: eng.dma_start(
                C_[slot][:, pc * 2048:(pc + 1) * 2048],
                dap(src, g * 512 + pc * 4 * 128 * D,
                    [[D, 128], [128 * D, 4], [1, 512]])),
                [(ps, 16)])
            c_piece[((knd, g), pc)] = (ps, ps.n)
        slot_uses[slot] += 1

    def ln_finalize(r_s1, r_s2, r_a, r_b, lo=0, n=16):
        def rng(base):
            return stcol[:, base + lo:base + lo + n]
        r.op("vector", lambda eng: eng.tensor_scalar(rng(r_a), rng(r_s1),
                                                     float(1.0 / D), None, ALU.mult),
             [(S_["vch"], 1)])
        r.op("vector", lambda eng: eng.tensor_scalar(rng(r_b), rng(r_s2),
                                                     float(1.0 / D), None, ALU.mult),
             [(S_["vch"], 1)])
        r.wait("vector", S_["vch"], S_["vch"].n)
        r.op("vector", lambda eng: eng.tensor_mul(rng(r_s1), rng(r_a), rng(r_a)),
             [(S_["vch"], 1)])
        r.wait("vector", S_["vch"], S_["vch"].n)
        r.op("vector", lambda eng: eng.tensor_sub(rng(r_s2), rng(r_b), rng(r_s1)),
             [(S_["vch"], 1)])
        r.wait("scalar", S_["vch"], S_["vch"].n)
        r.op("scalar", lambda eng: eng.activation(rng(r_b), rng(r_s2), AF.Sqrt,
                                                  bias=np.float32(EPS)), [(S_["sch"], 1)])
        r.wait("vector", S_["sch"], S_["sch"].n)
        r.op("vector", lambda eng: eng.reciprocal(rng(r_s1), rng(r_b)), [(S_["vch"], 1)])
        r.wait("vector", S_["vch"], S_["vch"].n)
        r.op("vector", lambda eng: eng.tensor_mul(rng(r_b), rng(r_a), rng(r_s1)),
             [(S_["vch"], 1)])
        r.wait("vector", S_["vch"], S_["vch"].n)
        r.op("vector", lambda eng: eng.tensor_scalar(rng(r_s2), rng(r_b),
                                                     -1.0, None, ALU.mult),
             [(S_["vch"], 1)])

    r.wait("tensor", S_["nrm"], NRM_TOTAL)
    opidx = 0
    LN1_FIN1 = LN1_FIN2 = None
    for g in range(4):
        swo = slot_of[("wo", g)]
        sxr = slot_of[("xr", g)]
        for qt in range(16):
            half = opidx % 2
            r.wait("tensor", S_["r1"], opidx - 1)
            for dc in range(16):
                if qt == 0 and dc % 4 == 0:
                    r.wait("tensor", *c_piece[(("wo", g), dc // 4)])
                incs = [(S_["op"], 1)] if dc == 15 else []
                r.op("tensor", lambda eng, g=g, qt=qt, dc=dc, half=half, swo=swo:
                     eng.matmul(psD[:, half * 512:(half + 1) * 512],
                                B_[dc // 4][:, (dc % 4) * 2048 + qt * 128:(dc % 4) * 2048 + (qt + 1) * 128],
                                C_[swo][:, dc * 512:(dc + 1) * 512],
                                start=(dc == 0), stop=(dc == 15)), incs)
            r.wait("vector", S_["op"], opidx + 1)
            if qt % 4 == 0:
                r.wait("vector", *c_piece[(("xr", g), qt // 4)])
            r.op("vector", lambda eng, g=g, qt=qt, half=half, sxr=sxr:
                 eng.tensor_add(A_[qt][:, g * 512:(g + 1) * 512],
                                psD[:, half * 512:(half + 1) * 512],
                                C_[sxr][:, qt * 512:(qt + 1) * 512]), [(S_["r1"], 1)])
            opidx += 1
            if g == 3:
                r.wait("vector", S_["r1"], 48 + qt + 1)
                r.op("vector", lambda eng, qt=qt: eng.reduce_sum(
                    stcol[:, qt:qt + 1], A_[qt][:, :], axis=AX.X), [(S_["st1"], 1)])
                r.wait("scalar", S_["r1"], 48 + qt + 1)
                r.wait("scalar", S_["sq1"], S_["sq1"].n)
                r.op("scalar", lambda eng, qt=qt: eng.activation(
                    X[:, :], A_[qt][:, :], AF.Square,
                    accum_out=stcol[:, 16 + qt:17 + qt]), [(S_["sq1"], 1)])
                if qt == 7:
                    r.wait("vector", S_["st1"], 8)
                    r.wait("vector", S_["sq1"], 8)
                    ln_finalize(0, 16, 32, 48, 0, 8)
                    LN1_FIN1 = S_["vch"].n
                if qt == 15:
                    r.wait("vector", S_["st1"], 16)
                    r.wait("vector", S_["sq1"], 16)
                    ln_finalize(0, 16, 32, 48, 8, 8)
                    LN1_FIN2 = S_["vch"].n

    sem, v = item_consumed(("wo", 3))
    r.wait("gpsimd", sem, v)
    r.op("gpsimd", lambda eng: eng.dma_start(
        C_[0][:, :], dap(ATd, 0, [[NE * RK, 128], [128 * NE * RK, 16], [1, 512]])),
        [(S_["wF"], 16)])
    sem, v = item_consumed(("xr", 3))
    r.wait("gpsimd", sem, v)
    r.op("gpsimd", lambda eng: eng.dma_start(
        C_[1][:, :], dap(BTd, 0, [[D, 128], [128 * D, 4], [1, 2048]])), [(S_["bc"], 16)])

    ps_t_last = [None, None]
    mh_txes_after = {}
    tx_state = [0]
    e8_count_at = {}

    def x1T_block(mc, qt):
        base = (mc % 4) * 2048 + qt * 128
        return B_[mc // 4][:, base:base + 128]

    def emit_normalize(qt):
        r.wait("vector", S_["vch"], LN1_FIN1 if qt < 8 else LN1_FIN2)
        r.op("vector", lambda eng, qt=qt: eng.tensor_scalar(
            A_[qt][:, :], A_[qt][:, :], stcol[:, qt:qt + 1], stcol[:, 16 + qt:17 + qt],
            ALU.mult, ALU.add), [(S_["n1"], 1)])

    def emit_x1tx(qt):
        r.wait("tensor", S_["n1"], qt + 1)
        for mc in range(16):
            bank = tx_state[0] % 2
            if ps_t_last[bank] is not None:
                sem, cntv = ps_t_last[bank]
                r.wait("tensor", sem, cntv)
            r.op("tensor", lambda eng, qt=qt, mc=mc, bank=bank:
                 eng.transpose(psT[:, bank * 1024:bank * 1024 + 128],
                               A_[qt][:, mc * 128:(mc + 1) * 128], identb[:, :]),
                 [(S_["tx"], 1)])
            r.wait("vector", S_["tx"], S_["tx"].n)
            r.op("vector", lambda eng, qt=qt, mc=mc, bank=bank:
                 eng.tensor_copy(x1T_block(mc, qt), psT[:, bank * 1024:bank * 1024 + 128]),
                 [(S_["txev"], 1)])
            ps_t_last[bank] = (S_["txev"], S_["txev"].n)
            tx_state[0] += 1

    def emit_h_router(qt):
        hhalf = qt % 2
        r.wait("tensor", S_["txev"], 16 * (qt + 1))
        if qt == 0:
            r.wait("tensor", S_["wF"], 16)
            r.wait("tensor", S_["bc"], 16)
            r.wait("tensor", S_["ld0g"], 16)
        r.wait("tensor", S_["mk"], qt - 1)
        r.wait("tensor", S_["lg"], qt - 1)
        for mc in range(16):
            r.op("tensor", lambda eng, qt=qt, mc=mc, hhalf=hhalf:
                 eng.matmul(psD[:, hhalf * 512:(hhalf + 1) * 512],
                            x1T_block(mc, qt), C_[0][:, mc * 512:(mc + 1) * 512],
                            start=(mc == 0), stop=(mc == 15)), [])
            r.op("tensor", lambda eng, qt=qt, mc=mc, hhalf=hhalf:
                 eng.matmul(psA[:, hhalf * 512:hhalf * 512 + 8],
                            x1T_block(mc, qt), wrtb[:, mc * 8:(mc + 1) * 8],
                            start=(mc == 0), stop=(mc == 15)),
                 [(S_["h"], 1)] if mc == 15 else [])

    def emit_router_chain(qt):
        hhalf = qt % 2
        lg8 = lgts[:, (qt % 2) * 8:(qt % 2) * 8 + 8]
        r.wait("vector", S_["h"], qt + 1)
        if qt == 0:
            r.wait("vector", S_["ld0"], lds0)
        if qt >= 2:
            r.wait("vector", S_["sch"], e8_count_at[qt - 2])
        r.op("vector", lambda eng, qt=qt, hhalf=hhalf, lg8=lg8:
             eng.tensor_add(lg8, psA[:, hhalf * 512:hhalf * 512 + 8], brs[:, :]),
             [(S_["lg"], 1)])
        if qt >= 1:
            r.wait("vector", S_["sch"], e8_count_at[qt - 1])
        r.wait("vector", S_["lg"], S_["lg"].n)
        r.op("vector", lambda eng, lg8=lg8: eng.reduce_max(smf[:, 16:17], lg8, axis=AX.X,
                                                           negate=True), [(S_["vch"], 1)])
        r.op("vector", lambda eng, lg8=lg8: eng.reduce_max(smf[:, 17:18], lg8, axis=AX.X),
             [(S_["vch"], 1)])
        r.wait("scalar", S_["vch"], S_["vch"].n)
        r.op("scalar", lambda eng, lg8=lg8: eng.activation(smf[:, 0:8], lg8, AF.Exp,
                                                           bias=smf[:, 16:17],
                                                           accum_out=smf[:, 18:19]),
             [(S_["sch"], 1)])
        e8_count_at[qt] = S_["sch"].n
        r.wait("vector", S_["sch"], S_["sch"].n)
        r.op("vector", lambda eng: eng.reciprocal(smf[:, 19:20], smf[:, 18:19]),
             [(S_["vch"], 1)])
        r.op("vector", lambda eng, lg8=lg8: eng.tensor_scalar(smf[:, 8:16], lg8,
                                                              smf[:, 17:18], None, ALU.is_ge),
             [(S_["vch"], 1)])
        r.wait("vector", S_["vch"], S_["vch"].n)
        r.op("vector", lambda eng, qt=qt: eng.tensor_scalar(coefT[:, qt * 8:(qt + 1) * 8],
                                                            smf[:, 8:16], smf[:, 19:20],
                                                            None, ALU.mult), [(S_["co"], 1)])

    def emit_mask(qt):
        hhalf = qt % 2
        r.wait("vector", S_["co"], qt + 1)
        if qt >= 2:
            r.wait("vector", S_["txes"], mh_txes_after[qt - 2])
        for ex in range(NE):
            r.op("vector", lambda eng, qt=qt, ex=ex, hhalf=hhalf:
                 eng.tensor_scalar(mh[:, hhalf * 512 + ex * 64: hhalf * 512 + (ex + 1) * 64],
                                   psD[:, hhalf * 512 + ex * 64: hhalf * 512 + (ex + 1) * 64],
                                   coefT[:, qt * 8 + ex:qt * 8 + ex + 1], None, ALU.mult),
                 [(S_["mk"], 1)] if ex == NE - 1 else [])

    def emit_mhtx(qt):
        hhalf = qt % 2
        r.wait("tensor", S_["mk"], qt + 1)
        for blk in range(4):
            bank = tx_state[0] % 2
            if ps_t_last[bank] is not None:
                sem, cntv = ps_t_last[bank]
                r.wait("tensor", sem, cntv)
            r.op("tensor", lambda eng, qt=qt, blk=blk, bank=bank, hhalf=hhalf:
                 eng.transpose(psT[:, bank * 1024:bank * 1024 + 128],
                               mh[:, hhalf * 512 + blk * 128: hhalf * 512 + (blk + 1) * 128],
                               identb[:, :]), [(S_["tx"], 1)])
            if qt == 0 and blk == 0:
                r.wait("scalar", S_["r1"], 48)
            r.wait("scalar", S_["tx"], S_["tx"].n)
            r.op("scalar", lambda eng, qt=qt, blk=blk, bank=bank:
                 eng.copy(C_[2][:, blk * 2048 + qt * 128: blk * 2048 + (qt + 1) * 128],
                          psT[:, bank * 1024:bank * 1024 + 128]), [(S_["txes"], 1)])
            ps_t_last[bank] = (S_["txes"], S_["txes"].n)
            tx_state[0] += 1
        mh_txes_after[qt] = S_["txes"].n

    def emit_y(qt):
        r.wait("tensor", S_["txes"], S_["txes"].n)
        for oc in range(4):
            yidx = qt * 4 + oc
            ybank = yidx % 2
            r.wait("tensor", S_["x2"], yidx - 1)
            for pair in range(4):
                r.op("tensor", lambda eng, qt=qt, oc=oc, pair=pair, ybank=ybank:
                     eng.matmul(psE[:, ybank * 512:(ybank + 1) * 512],
                                C_[2][:, pair * 2048 + qt * 128: pair * 2048 + (qt + 1) * 128],
                                C_[1][:, pair * 2048 + oc * 512: pair * 2048 + (oc + 1) * 512],
                                start=(pair == 0), stop=(pair == 3)),
                     [(S_["yy"], 1)] if pair == 3 else [])
            r.wait("vector", S_["yy"], yidx + 1)
            r.op("vector", lambda eng, qt=qt, oc=oc, ybank=ybank:
                 eng.scalar_tensor_tensor(A_[qt][:, oc * 512:(oc + 1) * 512],
                                          psE[:, ybank * 512:(ybank + 1) * 512],
                                          1.0 / 64.0, A_[qt][:, oc * 512:(oc + 1) * 512],
                                          ALU.mult, ALU.add), [(S_["x2"], 1)])

    def emit_ln2_stats(qt):
        r.wait("vector", S_["x2"], 4 * (qt + 1))
        r.op("vector", lambda eng, qt=qt: eng.reduce_sum(stcol[:, 32 + qt:33 + qt],
                                                         A_[qt][:, :], axis=AX.X),
             [(S_["st1"], 1)])
        r.wait("scalar", S_["x2"], 4 * (qt + 1))
        r.wait("scalar", S_["sq1"], S_["sq1"].n)
        r.op("scalar", lambda eng, qt=qt: eng.activation(X[:, :], A_[qt][:, :],
                                                         AF.Square,
                                                         accum_out=stcol[:, 48 + qt:49 + qt]),
             [(S_["sq1"], 1)])

    def emit_p5(qt, fin):
        r.wait("vector", S_["vch"], fin)
        r.wait("vector", S_["out"], 16 * (qt - 1))
        r.op("vector", lambda eng, qt=qt: eng.tensor_scalar(
            obuf[qt % 2][:, :], A_[qt][:, :], stcol[:, 32 + qt:33 + qt],
            stcol[:, 48 + qt:49 + qt], ALU.mult, ALU.add), [(S_["n2"], 1)])
        r.wait("sync", S_["n2"], qt + 1)
        r.wait("sync", S_["out"], 16 * qt)
        r.op("sync", lambda eng, qt=qt: eng.dma_start(y.ap()[qt * 128:(qt + 1) * 128, :],
                                                      obuf[qt % 2][:, :]), [(S_["out"], 16)])

    for qt in range(16):
        emit_normalize(qt)
    emit_x1tx(0)
    LN2_FIN1 = None
    for qt in range(16):
        emit_h_router(qt)
        emit_router_chain(qt)
        emit_mask(qt)
        if qt + 1 < 16:
            emit_x1tx(qt + 1)
        emit_mhtx(qt)
        emit_y(qt)
        emit_ln2_stats(qt)
        if qt == 7:
            r.wait("vector", S_["st1"], 24)
            r.wait("vector", S_["sq1"], 24)
            ln_finalize(32, 48, 0, 16, 0, 8)
            LN2_FIN1 = S_["vch"].n
        if qt >= 8:
            emit_p5(qt - 8, LN2_FIN1)
    r.wait("vector", S_["st1"], 32)
    r.wait("vector", S_["sq1"], 32)
    ln_finalize(32, 48, 0, 16, 8, 8)
    LN2_FIN2 = S_["vch"].n
    for qt in range(8, 16):
        emit_p5(qt, LN2_FIN2)
    r.wait("sync", S_["out"], 16 * 16)

    r.check()

    with nc.Block() as block:
        @block.tensor
        def _(eng):
            for f in r.ops["tensor"]:
                f(eng)

        @block.scalar
        def _(eng):
            for f in r.ops["scalar"]:
                f(eng)

        @block.vector
        def _(eng):
            for f in r.ops["vector"]:
                f(eng)

        @block.sync
        def _(eng):
            for f in r.ops["sync"]:
                f(eng)

        @block.gpsimd
        def _(eng):
            for f in r.ops["gpsimd"]:
                f(eng)

    ctx.close()
    return nc


_NC_CACHE = None


def _get_nc():
    global _NC_CACHE
    if _NC_CACHE is None:
        _NC_CACHE = build_nc()
    return _NC_CACHE


def make_in_maps(L0_attn_x, L0_attn_wq, L0_attn_wk, L0_attn_wv, L0_attn_wo,
                 L0_ffn_A, L0_ffn_B, L0_ffn_Wr, L0_ffn_br):
    x = np.asarray(L0_attn_x, dtype=np.float32)
    shared = {
        "wqT": np.ascontiguousarray(np.asarray(L0_attn_wq, np.float32).T),
        "wkT": np.ascontiguousarray(np.asarray(L0_attn_wk, np.float32).T),
        "wvT": np.ascontiguousarray(np.asarray(L0_attn_wv, np.float32).T),
        "woT": np.ascontiguousarray(np.asarray(L0_attn_wo, np.float32).T),
        "ATd": np.ascontiguousarray(
            np.asarray(L0_ffn_A, np.float32).transpose(2, 0, 1).reshape(D, NE * RK)),
        "BTd": np.ascontiguousarray(
            np.asarray(L0_ffn_B, np.float32).transpose(0, 2, 1).reshape(4 * 128, D)),
        "WrTd": np.ascontiguousarray(np.asarray(L0_ffn_Wr, np.float32).T),
        "brb": np.ascontiguousarray(
            np.broadcast_to(np.asarray(L0_ffn_br, np.float32).reshape(1, NE), (128, NE))),
        "identd": np.eye(128, dtype=ml_dtypes.bfloat16),
        "onesmd": np.ones((128, 128), dtype=np.float32),
        "onesrd": np.ones((1, 128), dtype=ml_dtypes.bfloat16),
        "onesmbd": np.ones((128, 128), dtype=ml_dtypes.bfloat16),
    }
    in_maps = []
    for c in range(8):
        b, hf = c // 2, c % 2
        own = x[b, hf * NQ:(hf + 1) * NQ, :]
        oth = x[b, (1 - hf) * NQ:(2 - hf) * NQ, :]
        m = dict(shared)
        m["xTb"] = np.ascontiguousarray(np.concatenate([own.T, oth.T], axis=1))
        m["xnat"] = np.ascontiguousarray(own)
        in_maps.append(m)
    return in_maps


def kernel(**inputs):
    in_maps = make_in_maps(**inputs)
    nc = _get_nc()
    res = run_bass_kernel_spmd(nc, in_maps, core_ids=list(range(8)))
    out = np.empty((4, 4096, 2048), dtype=np.float32)
    for c in range(8):
        b, hf = c // 2, c % 2
        out[b, hf * NQ:(hf + 1) * NQ, :] = res.results[c]["y"]
    return out


# version 16
# speedup vs baseline: 1.0363x; 1.0363x over previous
import numpy as np
import ml_dtypes
import concourse.bass as bass
import concourse.mybir as mybir
from concourse.bass_utils import run_bass_kernel_spmd

F32 = mybir.dt.float32
BF16 = mybir.dt.bfloat16
AF = mybir.ActivationFunctionType
ALU = mybir.AluOpType
AX = mybir.AxisListType

D = 2048
S = 4096
NQ = 2048
H = 16
HD = 128
NE = 8
RK = 64
BETA = 9.5
EPS = 1e-5

ENGS = ("tensor", "scalar", "vector", "sync", "gpsimd")


class Sem:
    def __init__(self, h, name):
        self.h = h
        self.name = name
        self.n = 0


class Rec:
    def __init__(self):
        self.ops = {e: [] for e in ENGS}
        self.waits = []

    def wait(self, e, sem, val):
        if val <= 0:
            return
        self.waits.append((sem, val))
        self.ops[e].append(lambda eng, s=sem, v=val: eng.wait_ge(s.h, v))

    def op(self, e, build, incs=()):
        incs = tuple(incs)
        for s, a in incs:
            s.n += a
        assert len(incs) <= 1, "one sem update per instruction"
        attach, spill = incs[:1], incs[1:]

        def f(eng, build=build, attach=attach, spill=spill):
            inst = build(eng)
            for s, a in attach:
                inst = inst.then_inc(s.h, a)
            for s, a in spill:
                eng.sem_inc(s.h, a)
        self.ops[e].append(f)

    def check(self):
        for s, v in self.waits:
            assert v <= s.n, f"wait on {s.name} for {v} but total incs {s.n}"


def dap(t, off, dims):
    return bass.AP(t, off, [list(d) for d in dims])


def build_nc():
    nc = bass.Bass("TRN2", target_bir_lowering=False)

    xTb = nc.dram_tensor("xTb", [D, NQ], F32, kind="ExternalInput")
    xnat = nc.dram_tensor("xnat", [NQ, D], F32, kind="ExternalInput")
    wqT = nc.dram_tensor("wqT", [D, D], F32, kind="ExternalInput")
    wkT = nc.dram_tensor("wkT", [D, D], F32, kind="ExternalInput")
    wvT = nc.dram_tensor("wvT", [D, D], F32, kind="ExternalInput")
    woT = nc.dram_tensor("woT", [D, D], F32, kind="ExternalInput")
    ATd = nc.dram_tensor("ATd", [D, NE * RK], F32, kind="ExternalInput")
    BTd = nc.dram_tensor("BTd", [4 * 128, D], F32, kind="ExternalInput")
    WrTd = nc.dram_tensor("WrTd", [D, NE], F32, kind="ExternalInput")
    brb = nc.dram_tensor("brb", [128, NE], F32, kind="ExternalInput")
    identd = nc.dram_tensor("identd", [128, 128], BF16, kind="ExternalInput")
    onesmd = nc.dram_tensor("onesmd", [128, 128], F32, kind="ExternalInput")
    onesmbd = nc.dram_tensor("onesmbd", [128, 128], BF16, kind="ExternalInput")
    onesrd = nc.dram_tensor("onesrd", [1, 128], BF16, kind="ExternalInput")

    y = nc.dram_tensor("y", [NQ, D], F32, kind="ExternalOutput")

    kdram = nc.dram_tensor("kdram", [8, 128, 8192], BF16)
    kown = nc.dram_tensor("kown", [4, 128, 8192], BF16)
    vown = nc.dram_tensor("vown", [NQ, D], BF16)
    vdram = nc.dram_tensor("vdram", [S, D], BF16)
    qdram = nc.dram_tensor("qdram", [4, 128, 8192], BF16)
    negdram = nc.dram_tensor("negdram", [16, 2048], BF16)

    from contextlib import ExitStack
    ctx = ExitStack()
    e = ctx.enter_context

    AB = [e(nc.sbuf_tensor(f"AB{i}", [128, 8192], BF16)) for i in range(4)]
    A_ = [AB[i // 4][:, (i % 4) * 2048:(i % 4 + 1) * 2048] for i in range(16)]
    B_ = [e(nc.sbuf_tensor(f"B{i}", [128, 8192], BF16)) for i in range(4)]
    C_ = [e(nc.sbuf_tensor(f"C{i}", [128, 8192], BF16)) for i in range(3)]
    obuf = [e(nc.sbuf_tensor(f"obuf{i}", [128, 2048], F32)) for i in range(2)]
    mh = e(nc.sbuf_tensor("mh", [128, 1024], BF16))
    coefT = e(nc.sbuf_tensor("coefT", [128, 128], F32))
    stcol = e(nc.sbuf_tensor("stcol", [128, 64], F32))
    smf = e(nc.sbuf_tensor("smf", [128, 32], F32))
    lgts = e(nc.sbuf_tensor("lgts", [128, 16], F32))
    wrtb = e(nc.sbuf_tensor("wrtb", [128, 128], BF16))
    brs = e(nc.sbuf_tensor("brs", [128, NE], F32))
    identb = e(nc.sbuf_tensor("identb", [128, 128], BF16))
    onesm = e(nc.sbuf_tensor("onesm", [128, 128], F32))
    onesr = e(nc.sbuf_tensor("onesr", [1, 128], BF16))
    onesmb = e(nc.sbuf_tensor("onesmb", [128, 128], BF16))
    acc2 = e(nc.sbuf_tensor("acc2", [128, 2048], BF16))
    pa = e(nc.sbuf_tensor("pa", [128, 32], F32))
    pab = e(nc.sbuf_tensor("pab", [128, 16], BF16))
    X = e(nc.sbuf_tensor("X", [128, 2048], BF16))

    psA = e(nc.psum_tensor("psA", [128, 1024], F32))
    psB_cm = nc.psum_tensor("psB", [128, 1024], F32)
    psB = psB_cm.__enter__()
    psC_cm = nc.psum_tensor("psC", [128, 1024], F32)
    psC = psC_cm.__enter__()
    psF_cm = nc.psum_tensor("psF", [128, 1024], F32)
    psF = psF_cm.__enter__()

    semnames = ("wA wAf x0 x1 xf0 xf1 mmP1 evP1 sp kv0 kv1 kvg0 kvg1 sc exp avf "
                "bc nrm c0 c1 c2 op r1 st1 sq1 vch sch n1 tx "
                "ccs asc ard stq xrg xr csd dnb rcp lnr xg0 xg1 xg2 xg3 xg4 xg5 xg6 xg7 "
                "txev txes wF h lg co mk yy x2 n2 out ld0 ld0g").split()
    S_ = {n: Sem(e(nc.semaphore(f"s_{n}")), n) for n in semnames}

    r = Rec()

    for dst, src in ((identb, identd), (onesm, onesmd), (onesmb, onesmbd), (onesr, onesrd), (brs, brb)):
        r.wait("sync", S_["ld0"], S_["ld0"].n)
        r.op("sync", lambda eng, d=dst, s=src: eng.dma_start(d[:, :], s.ap()[:, :]),
             [(S_["ld0"], 16)])
    r.op("gpsimd", lambda eng: eng.dma_start(
        wrtb[:, :], dap(WrTd, 0, [[NE, 128], [128 * NE, 16], [1, NE]])), [(S_["ld0g"], 16)])
    lds0 = S_["ld0"].n

    def pool6(i):
        i = i % 6
        if i < 2:
            return psB[:, i * 512:(i + 1) * 512]
        if i < 4:
            return psC[:, (i - 2) * 512:(i - 2 + 1) * 512]
        return psF[:, (i - 4) * 512:(i - 4 + 1) * 512]

    g_grp = 0
    n_spills = 0
    vslot_i = 0
    stage_last_spill = {}
    xcnk_free_at = {}
    pass_end_mm = []

    passes = (("k", wkT, 4, kown), ("v", wvT, 4, vown), ("q", wqT, 4, qdram))
    x_after = {}

    def emit_xchunk(t):
        par = t % 2
        if par in xcnk_free_at:
            r.wait("gpsimd", S_["mmP1"], xcnk_free_at[par])
        r.op("gpsimd", lambda eng, par=par, t=t: eng.dma_start(
            C_[par][:, :], dap(xTb, t * 512, [[NQ, 128], [128 * NQ, 16], [1, 512]])),
            [(S_["x" + str(par)], 16)])
        x_after[t] = S_["x" + str(par)].n

    for p_i, (kind, wT, nchunks, dest) in enumerate(passes):
        emit_xchunk(0)
        w_after = {}
        for g4 in range(4):
            wsem = S_["wA"] if g4 % 2 == 0 else S_["wAf"]
            if p_i > 0:
                r.wait("gpsimd", S_["mmP1"],
                       pass_end_mm[p_i - 1] - 16 + 4 * (g4 + 1))
            r.wait("gpsimd", wsem, wsem.n)
            r.op("gpsimd", lambda eng, g4=g4, wT=wT: eng.dma_start(
                AB[g4][:, :],
                dap(wT, g4 * 512, [[D, 128], [128 * D, 16], [1, 512]])),
                [(wsem, 16)])
            w_after[g4] = (wsem, wsem.n)

        for t in range(nchunks):
            par = t % 2
            if t + 1 < nchunks:
                emit_xchunk(t + 1)
            if kind == "q" and t == 1:
                r.wait("gpsimd", S_["sp"], n_spills * 16)
                r.op("gpsimd", lambda eng: eng.collective_compute(
                    "AllGather", ALU.bypass,
                    replica_groups=[[0, 1], [2, 3], [4, 5], [6, 7]],
                    ins=[kown.ap().opt()], outs=[kdram.ap().opt()]),
                    [(S_["ccs"], 1)])
                r.op("gpsimd", lambda eng: eng.collective_compute(
                    "AllGather", ALU.bypass,
                    replica_groups=[[0, 1], [2, 3], [4, 5], [6, 7]],
                    ins=[vown.ap().opt()], outs=[vdram.ap().opt()]),
                    [(S_["ccs"], 1)])
            r.wait("tensor", S_["x" + str(par)], x_after[t])

            if kind in ("k", "q"):
                for dk in range(16):
                    if dk % 4 == 0:
                        r.wait("tensor", *w_after[dk // 4])
                    slot = g_grp % 6
                    r.wait("tensor", S_["evP1"], g_grp - 5)
                    for mc in range(16):
                        incs = [(S_["mmP1"], 1)] if mc == 15 else []
                        r.op("tensor", lambda eng, slot=slot, mc=mc, dk=dk, par=par:
                             eng.matmul(pool6(slot),
                                        AB[dk // 4][:, mc * 512 + (dk % 4) * 128:
                                                    mc * 512 + (dk % 4) * 128 + 128],
                                        C_[par][:, mc * 512:(mc + 1) * 512],
                                        start=(mc == 0), stop=(mc == 15)), incs)
                    r.wait("scalar", S_["mmP1"], S_["mmP1"].n)
                    if dk == 0 and ("kq", par) in stage_last_spill:
                        r.wait("scalar", S_["sp"], 16 * stage_last_spill[("kq", par)])
                    r.op("scalar", lambda eng, slot=slot, dk=dk, par=par:
                         eng.copy(B_[par][:, dk * 512:(dk + 1) * 512], pool6(slot)),
                         [(S_["evP1"], 1)])
                    g_grp += 1
                r.wait("sync", S_["evP1"], S_["evP1"].n)
                r.wait("sync", S_["sp"], S_["sp"].n)
                r.op("sync", lambda eng, t=t, par=par, dest=dest:
                     eng.dma_start(dest.ap()[t, :, :], B_[par][:, :]), [(S_["sp"], 16)])
                n_spills += 1
                stage_last_spill[("kq", par)] = n_spills
            else:
                for dvc in range(4):
                    r.wait("tensor", *w_after[dvc])
                    for tt in range(4):
                        vs = (t % 2) * 4 + tt
                        tt_global = t * 4 + tt
                        slot = g_grp % 6
                        r.wait("tensor", S_["evP1"], g_grp - 5)
                        for mc in range(16):
                            incs = [(S_["mmP1"], 1)] if mc == 15 else []
                            r.op("tensor", lambda eng, slot=slot, mc=mc, tt=tt,
                                 dvc=dvc, par=par:
                                 eng.matmul(pool6(slot),
                                            C_[par][:, mc * 512 + tt * 128:
                                                    mc * 512 + (tt + 1) * 128],
                                            AB[dvc][:, mc * 512:(mc + 1) * 512],
                                            start=(mc == 0), stop=(mc == 15)), incs)
                        r.wait("scalar", S_["mmP1"], S_["mmP1"].n)
                        if dvc == 0 and ("v", vs) in stage_last_spill:
                            r.wait("scalar", S_["sp"], 16 * stage_last_spill[("v", vs)])
                        r.op("scalar", lambda eng, slot=slot, vs=vs, dvc=dvc:
                             eng.copy(B_[2 + vs // 4][:, (vs % 4) * 2048 + dvc * 512:
                                                      (vs % 4) * 2048 + (dvc + 1) * 512],
                                      pool6(slot)), [(S_["evP1"], 1)])
                        g_grp += 1
                        if dvc == 3:
                            r.wait("sync", S_["evP1"], S_["evP1"].n)
                            r.wait("sync", S_["sp"], S_["sp"].n)
                            r.op("sync", lambda eng, tt_global=tt_global, vs=vs:
                                 eng.dma_start(
                                     vown.ap()[tt_global * 128:(tt_global + 1) * 128, :],
                                     B_[2 + vs // 4][:, (vs % 4) * 2048:(vs % 4 + 1) * 2048]),
                                 [(S_["sp"], 16)])
                            n_spills += 1
                            stage_last_spill[("v", vs)] = n_spills
            xcnk_free_at[par] = S_["mmP1"].n
        pass_end_mm.append(S_["mmP1"].n)

    P1_SPILLS = S_["sp"].n
    P1_MM = S_["mmP1"].n
    EV_TOT = S_["evP1"].n

    def ktbuf(par, piece):
        return A_[2 * par + piece]

    def vbuf(par, piece):
        return A_[4 + 2 * par + piece]

    def qbuf(par):
        return A_[8 + par]

    def ring(i):
        return A_[10 + (i % 10) // 2][:, ((i % 10) % 2) * 1024:(((i % 10) % 2) + 1) * 1024]

    def attnT(vh, lo, width):
        hh, qh = vh // 2, vh % 2
        base = (hh % 4) * 2048 + qh * 1024 + lo
        return B_[hh // 4][:, base:base + width]

    def chslot(vh):
        return acc2[:, (vh % 2) * 1024:(vh % 2 + 1) * 1024]

    def pslot(vh):
        return acc2[:, ((vh + 1) % 2) * 1024:((vh + 1) % 2 + 1) * 1024]

    NVH = 32
    kv_snap = {}
    kvv_snap = {}

    def emit_loads(hh):
        par = hh % 2
        if hh < 2:
            r.wait("sync", S_["sp"], P1_SPILLS)
            r.wait("sync", S_["mmP1"], P1_MM)
            r.wait("sync", S_["ccs"], 2)
        else:
            r.wait("sync", S_["sc"], 64 * (hh - 1))
            r.wait("sync", S_["avf"], 64 * (hh - 1))
        ks = S_["kv" + str(par)]
        for piece in range(2):
            r.wait("sync", ks, ks.n)
            r.op("sync", lambda eng, hh=hh, g=piece, par=par: eng.dma_start(
                ktbuf(par, g)[:, :],
                dap(kdram, (4 * g) * 128 * 8192 + hh * 512,
                    [[8192, 128], [128 * 8192, 4], [1, 512]])), [(ks, 16)])
        r.wait("sync", ks, ks.n)
        r.op("sync", lambda eng, hh=hh, par=par: eng.dma_start(
            qbuf(par)[:, :],
            dap(qdram, hh * 512, [[8192, 128], [128 * 8192, 4], [1, 512]])),
            [(ks, 16)])
        kv_snap[hh] = ks.n
        for piece in range(2):
            r.wait("sync", ks, ks.n)
            r.op("sync", lambda eng, hh=hh, g=piece, par=par: eng.dma_start(
                vbuf(par, g)[:, :],
                dap(vdram, (g * 16 * 128) * D + hh * 128,
                    [[D, 128], [128 * D, 16], [1, 128]])), [(ks, 16)])
        kvv_snap[hh] = ks.n

    turns = []
    turn_of_kt = {}
    turn_of_den = {}
    for vh in range(NVH):
        for kt in range(32):
            if kt == 4 and vh >= 1:
                turn_of_den[vh - 1] = len(turns)
                turns.append(("den", vh - 1))
            turn_of_kt[(vh, kt)] = len(turns)
            turns.append(("kt", vh, kt))
    turn_of_den[31] = len(turns)
    turns.append(("den", 31))

    def turn_slot(T):
        return psA if T % 2 == 0 else psC

    def turn_consumed(T):
        t = turns[T]
        if t[0] == "kt":
            return (S_["exp"], t[1] * 32 + t[2] + 1)
        return (S_["lnr"], t[1] + 1)

    def emit_passA_mm(ga):
        vh, a = ga // 64, ga % 64
        hh, qh, par = vh // 2, vh % 2, (vh // 2) % 2
        j, c = a // 8, a % 8
        if a == 0:
            r.wait("tensor", S_["kv" + str(par)], kv_snap[hh])
        if ga == 0:
            r.wait("tensor", S_["evP1"], EV_TOT)
        r.wait("tensor", S_["ard"], ga - 1)
        r.op("tensor", lambda eng, ga=ga, qh=qh, par=par, j=j, c=c: eng.matmul(
            psF[:, (ga % 2) * 512:(ga % 2 + 1) * 512],
            qbuf(par)[:, qh * 1024 + j * 128:qh * 1024 + (j + 1) * 128],
            ktbuf(par, c // 4)[:, (c % 4) * 512:(c % 4 + 1) * 512],
            start=True, stop=True), [(S_["asc"], 1)])

    def emit_passA_dve(ga):
        vh, a = ga // 64, ga % 64
        j, c = a // 8, a % 8
        scc = (j % 2) * 8 + c
        r.wait("vector", S_["asc"], ga + 1)
        r.op("vector", lambda eng, ga=ga, scc=scc: eng.reduce_max(
            pa[:, scc:scc + 1], psF[:, (ga % 2) * 512:(ga % 2 + 1) * 512],
            axis=AX.X), [(S_["ard"], 1)])
        if c == 7:
            stc = (vh % 2) * 8 + j
            r.wait("vector", S_["ard"], S_["ard"].n)
            if vh >= 2:
                r.wait("vector", S_["xg" + str(j)], 16 * (vh - 1))
            r.op("vector", lambda eng, j=j, stc=stc: eng.reduce_max(
                pab[:, stc:stc + 1], pa[:, (j % 2) * 8:(j % 2) * 8 + 8],
                axis=AX.X, negate=True), [(S_["stq"], 1)])

    def emit_stat_dma(vh, j):
        stc = (vh % 2) * 8 + j
        qh = vh % 2
        xg = S_["xg" + str(j)]
        if vh == 0 and j == 0:
            r.wait("gpsimd", S_["sp"], P1_SPILLS)
        r.wait("gpsimd", xg, 16 * vh)
        if vh >= 2 and j == 0:
            r.wait("gpsimd", S_["sc"], 32 * (vh - 1))
        r.wait("gpsimd", S_["stq"], 8 * vh + j + 1)
        r.op("gpsimd", lambda eng, j=j, qh=qh, stc=stc: eng.dma_start(
            X[32 * qh:32 * qh + 1, qh * 1024 + j * 128:qh * 1024 + (j + 1) * 128],
            pab[:, stc:stc + 1]), [(xg, 16)])

    def casc_ops(vh, t):
        CH, P = (lambda: chslot(vh)), (lambda: pslot(vh))
        R = lambda k: (lambda: ring(32 * vh + k))
        if t == 5:
            return [(CH, R(0), R(1), [0, 1])]
        if 6 <= t <= 11:
            return [(CH, CH, R(t - 4), [t - 4])]
        if t == 12:
            return [(P, R(8), R(9), [8, 9])]
        if 13 <= t <= 18:
            return [(P, P, R(t - 3), [t - 3])]
        if t == 19:
            return [(P, P, CH, [])]
        if t == 20:
            return [(CH, R(16), R(17), [16, 17])]
        if 21 <= t <= 26:
            return [(CH, CH, R(t - 3), [t - 3])]
        if t == 27:
            return [(P, P, CH, [])]
        if t == 28:
            return [(CH, R(24), R(25), [24, 25])]
        if 29 <= t <= 33:
            return [(CH, CH, R(t - 3), [t - 3])]
        if t == 34:
            return [(CH, CH, R(31), [31])]
        if t == 35:
            return [(P, P, CH, [])]
        return []

    csd_at = {}
    csd_final = {}

    def emit_cascade(vh, t):
        for dst, s0, s1, cons in casc_ops(vh, t):
            if cons:
                r.wait("vector", S_["exp"], 32 * vh + max(cons) + 1)
            if t == 5:
                r.wait("vector", S_["dnb"], 2 * vh)
            r.wait("vector", S_["csd"], S_["csd"].n)
            r.op("vector", lambda eng, dst=dst, s0=s0, s1=s1: eng.tensor_add(
                dst(), s0(), s1()), [(S_["csd"], 1)])
            for k in cons:
                csd_at[32 * vh + k] = S_["csd"].n
            if t == 35:
                csd_final[vh] = S_["csd"].n

    def emit_den(vhd):
        T = turn_of_den[vhd]
        slot = turn_slot(T)
        if T >= 2:
            sem, v = turn_consumed(T - 2)
            r.wait("tensor", sem, v)
        r.wait("tensor", S_["exp"], 32 * (vhd + 1))
        r.wait("tensor", S_["csd"], csd_final[vhd])
        for qc in range(2):
            r.op("tensor", lambda eng, qc=qc, vhd=vhd, slot=slot: eng.matmul(
                slot[:, qc * 512:(qc + 1) * 512], onesmb[:, :],
                pslot(vhd)[:, qc * 512:(qc + 1) * 512],
                start=True, stop=True), [(S_["dnb"], 1)])

    def emit_recip(vhd):
        slot = turn_slot(turn_of_den[vhd])
        r.wait("scalar", S_["dnb"], 2 * (vhd + 1))
        r.wait("scalar", S_["rcp"], vhd - 1)
        r.op("scalar", lambda eng, vhd=vhd, slot=slot: eng.activation(
            obuf[0][:, (vhd % 2) * 1024:(vhd % 2 + 1) * 1024],
            slot[:, 0:1024], AF.Ln), [(S_["lnr"], 1)])
        r.wait("scalar", S_["lnr"], S_["lnr"].n)
        r.op("scalar", lambda eng, vhd=vhd: eng.activation(
            obuf[1][:, (vhd % 2) * 1024:(vhd % 2 + 1) * 1024],
            obuf[0][:, (vhd % 2) * 1024:(vhd % 2 + 1) * 1024],
            AF.Exp, scale=np.float32(-1.0)), [(S_["rcp"], 1)])

    def emit_evac(vhd, h):
        r.wait("vector", S_["rcp"], vhd + 1)
        r.wait("vector", S_["avf"], 32 * (vhd + 1))
        if vhd == 0 and h == 0:
            r.wait("vector", S_["sp"], P1_SPILLS)
        r.op("vector", lambda eng, h=h, vhd=vhd: eng.tensor_mul(
            attnT(vhd, h * 512, 512), psB[:, h * 512:(h + 1) * 512],
            obuf[1][:, (vhd % 2) * 1024 + h * 512:(vhd % 2) * 1024 + (h + 1) * 512]),
            [(S_["nrm"], 1)])

    def emit_expB(vh, kt):
        gk = 32 * vh + kt
        T = turn_of_kt[(vh, kt)]
        slot = turn_slot(T)
        r.wait("scalar", S_["sc"], gk + 1)
        if gk >= 10:
            r.wait("scalar", S_["avf"], gk - 9)
            r.wait("scalar", S_["csd"], csd_at[gk - 10])
        r.op("scalar", lambda eng, gk=gk, slot=slot: eng.activation(
            ring(gk), slot[:, :], AF.Exp), [(S_["exp"], 1)])

    def emit_scoresB(vh, kt):
        gk = 32 * vh + kt
        T = turn_of_kt[(vh, kt)]
        slot = turn_slot(T)
        qh, par = vh % 2, (vh // 2) % 2
        if kt == 0:
            for j in range(8):
                r.wait("tensor", S_["xg" + str(j)], 16 * (vh + 1))
        if T >= 2:
            sem, v = turn_consumed(T - 2)
            r.wait("tensor", sem, v)
        piece, tile = kt // 16, kt % 16
        for qc in range(2):
            r.op("tensor", lambda eng, qc=qc, qh=qh, slot=slot: eng.matmul(
                slot[:, qc * 512:(qc + 1) * 512], onesmb[32 * qh:32 * qh + 1, :],
                X[32 * qh:32 * qh + 1, qh * 1024 + qc * 512:qh * 1024 + (qc + 1) * 512],
                start=True, stop=False), [])
        for qc in range(2):
            r.op("tensor", lambda eng, piece=piece, tile=tile, qc=qc, par=par, qh=qh,
                 slot=slot:
                 eng.matmul(slot[:, qc * 512:(qc + 1) * 512],
                            ktbuf(par, piece)[:, tile * 128:(tile + 1) * 128],
                            qbuf(par)[:, qh * 1024 + qc * 512:qh * 1024 + (qc + 1) * 512],
                            start=False, stop=True),
                 [(S_["sc"], 1)] if qc == 1 else [])

    av_q = []
    for vhV in range(NVH):
        for ktV in range(32):
            av_q.append((vhV * 32 + max(9, ktV + 2), vhV, ktV))
    av_q.sort()

    def emit_av_pops(p, limit=2):
        n_av = 0
        while av_q and av_q[0][0] <= p and n_av < limit:
            _, vhV, ktV = av_q.pop(0)
            parV = (vhV // 2) % 2
            gkV = vhV * 32 + ktV
            if ktV == 0:
                r.wait("tensor", S_["kv" + str(parV)], kvv_snap[vhV // 2])
            r.wait("tensor", S_["exp"], gkV + 1)
            piece, tile = ktV // 16, ktV % 16
            for qc in range(2):
                if ktV == 0:
                    r.wait("tensor", S_["nrm"], 2 * (vhV - 1) + qc + 1)
                r.op("tensor", lambda eng, piece=piece, tile=tile, qc=qc, parV=parV,
                     gkV=gkV, ktV=ktV:
                     eng.matmul(psB[:, qc * 512:(qc + 1) * 512],
                                vbuf(parV, piece)[:, tile * 128:(tile + 1) * 128],
                                ring(gkV)[:, qc * 512:(qc + 1) * 512],
                                start=(ktV == 0), stop=(ktV == 31)),
                     [(S_["avf"], 1)] if qc == 1 else [])
            n_av += 1

    emit_loads(0)
    emit_loads(1)
    r.wait("tensor", S_["ld0"], lds0)
    for ga in range(96):
        emit_passA_mm(ga)
        emit_passA_dve(ga)
        if ga == 63:
            for j in range(8):
                emit_stat_dma(0, j)
    for j in range(4):
        emit_stat_dma(1, j)

    NPOS = NVH * 32
    for p in range(NPOS):
        vh, kt = p // 32, p % 32

        for hh in range(2, 16):
            if p == 64 * (hh - 1):
                emit_loads(hh)
        for j in range(4, 8):
            if p == 4 * (j - 4) + 4:
                emit_stat_dma(1, j)
        for vhg in range(2, NVH):
            for j in range(8):
                if p == 32 * (vhg - 2) + 20 + 4 * j:
                    emit_stat_dma(vhg, j)

        if kt == 4 and vh >= 1:
            emit_den(vh - 1)
        emit_scoresB(vh, kt)
        ga0 = 2 * (p + 48)
        if ga0 + 1 < 2048:
            emit_passA_mm(ga0)
            emit_passA_mm(ga0 + 1)
        emit_av_pops(p)

        if kt == 5 and vh >= 1:
            emit_recip(vh - 1)
        emit_expB(vh, kt)

        if kt in (0, 1, 2, 3) and vh >= 1:
            emit_cascade(vh - 1, 32 + kt)
        if kt == 6 and vh >= 1:
            emit_evac(vh - 1, 0)
        if kt == 7 and vh >= 1:
            emit_evac(vh - 1, 1)
        if ga0 + 1 < 2048:
            emit_passA_dve(ga0)
            emit_passA_dve(ga0 + 1)
        emit_cascade(vh, kt)

    emit_av_pops(10 ** 9, limit=64)
    for t in (32, 33, 34, 35):
        emit_cascade(31, t)
    emit_den(31)
    emit_recip(31)
    emit_evac(31, 0)
    emit_evac(31, 1)
    NRM_TOTAL = S_["nrm"].n

    psF_cm.__exit__(None, None, None)
    psC_cm.__exit__(None, None, None)
    psB_cm.__exit__(None, None, None)
    psD = e(nc.psum_tensor("psD", [128, 1024], F32))
    psT = e(nc.psum_tensor("psT", [128, 2048], BF16))
    psE = e(nc.psum_tensor("psE", [128, 1024], F32))

    slot_items = []
    for g in range(4):
        slot_items.append(("wo", g))
        slot_items.append(("xr", g))
    slot_of = {}
    slot_uses = [0, 0, 0]
    for i, item in enumerate(slot_items):
        slot_of[item] = i % 3

    def item_consumed(item):
        knd, g = item
        return (S_["op"], 16 * (g + 1)) if knd == "wo" else (S_["r1"], 16 * (g + 1))

    for i, (knd, g) in enumerate(slot_items):
        slot = i % 3
        if slot_uses[slot] == 0:
            r.wait("gpsimd", S_["mmP1"], P1_MM)
        else:
            sem, v = item_consumed(slot_items[i - 3])
            r.wait("gpsimd", sem, v)
        src = woT if knd == "wo" else xnat
        r.op("gpsimd", lambda eng, slot=slot, g=g, src=src: eng.dma_start(
            C_[slot][:, :], dap(src, g * 512, [[D, 128], [128 * D, 16], [1, 512]])),
            [(S_["c" + str(slot)], 16)])
        slot_uses[slot] += 1
    c_counts = {}
    cnt = [0, 0, 0]
    for i, item in enumerate(slot_items):
        slot = i % 3
        cnt[slot] += 16
        c_counts[item] = cnt[slot]

    def ln_finalize(r_s1, r_s2, r_a, r_b, lo=0, n=16):
        def rng(base):
            return stcol[:, base + lo:base + lo + n]
        r.op("vector", lambda eng: eng.tensor_scalar(rng(r_a), rng(r_s1),
                                                     float(1.0 / D), None, ALU.mult),
             [(S_["vch"], 1)])
        r.op("vector", lambda eng: eng.tensor_scalar(rng(r_b), rng(r_s2),
                                                     float(1.0 / D), None, ALU.mult),
             [(S_["vch"], 1)])
        r.wait("vector", S_["vch"], S_["vch"].n)
        r.op("vector", lambda eng: eng.tensor_mul(rng(r_s1), rng(r_a), rng(r_a)),
             [(S_["vch"], 1)])
        r.wait("vector", S_["vch"], S_["vch"].n)
        r.op("vector", lambda eng: eng.tensor_sub(rng(r_s2), rng(r_b), rng(r_s1)),
             [(S_["vch"], 1)])
        r.wait("scalar", S_["vch"], S_["vch"].n)
        r.op("scalar", lambda eng: eng.activation(rng(r_b), rng(r_s2), AF.Sqrt,
                                                  bias=np.float32(EPS)), [(S_["sch"], 1)])
        r.wait("vector", S_["sch"], S_["sch"].n)
        r.op("vector", lambda eng: eng.reciprocal(rng(r_s1), rng(r_b)), [(S_["vch"], 1)])
        r.wait("vector", S_["vch"], S_["vch"].n)
        r.op("vector", lambda eng: eng.tensor_mul(rng(r_b), rng(r_a), rng(r_s1)),
             [(S_["vch"], 1)])
        r.wait("vector", S_["vch"], S_["vch"].n)
        r.op("vector", lambda eng: eng.tensor_scalar(rng(r_s2), rng(r_b),
                                                     -1.0, None, ALU.mult),
             [(S_["vch"], 1)])

    r.wait("tensor", S_["nrm"], NRM_TOTAL)
    opidx = 0
    LN1_FIN1 = LN1_FIN2 = None
    for g in range(4):
        swo = slot_of[("wo", g)]
        sxr = slot_of[("xr", g)]
        r.wait("tensor", S_["c" + str(swo)], c_counts[("wo", g)])
        for qt in range(16):
            half = opidx % 2
            r.wait("tensor", S_["r1"], opidx - 1)
            for dc in range(16):
                incs = [(S_["op"], 1)] if dc == 15 else []
                r.op("tensor", lambda eng, g=g, qt=qt, dc=dc, half=half, swo=swo:
                     eng.matmul(psD[:, half * 512:(half + 1) * 512],
                                B_[dc // 4][:, (dc % 4) * 2048 + qt * 128:(dc % 4) * 2048 + (qt + 1) * 128],
                                C_[swo][:, dc * 512:(dc + 1) * 512],
                                start=(dc == 0), stop=(dc == 15)), incs)
            r.wait("vector", S_["op"], opidx + 1)
            if qt == 0:
                r.wait("vector", S_["c" + str(sxr)], c_counts[("xr", g)])
            r.op("vector", lambda eng, g=g, qt=qt, half=half, sxr=sxr:
                 eng.tensor_add(A_[qt][:, g * 512:(g + 1) * 512],
                                psD[:, half * 512:(half + 1) * 512],
                                C_[sxr][:, qt * 512:(qt + 1) * 512]), [(S_["r1"], 1)])
            opidx += 1
            if g == 3:
                r.wait("vector", S_["r1"], 48 + qt + 1)
                r.op("vector", lambda eng, qt=qt: eng.reduce_sum(
                    stcol[:, qt:qt + 1], A_[qt][:, :], axis=AX.X), [(S_["st1"], 1)])
                r.wait("scalar", S_["r1"], 48 + qt + 1)
                r.wait("scalar", S_["sq1"], S_["sq1"].n)
                r.op("scalar", lambda eng, qt=qt: eng.activation(
                    X[:, :], A_[qt][:, :], AF.Square,
                    accum_out=stcol[:, 16 + qt:17 + qt]), [(S_["sq1"], 1)])
                if qt == 7:
                    r.wait("vector", S_["st1"], 8)
                    r.wait("vector", S_["sq1"], 8)
                    ln_finalize(0, 16, 32, 48, 0, 8)
                    LN1_FIN1 = S_["vch"].n
                if qt == 15:
                    r.wait("vector", S_["st1"], 16)
                    r.wait("vector", S_["sq1"], 16)
                    ln_finalize(0, 16, 32, 48, 8, 8)
                    LN1_FIN2 = S_["vch"].n

    sem, v = item_consumed(("wo", 3))
    r.wait("gpsimd", sem, v)
    r.op("gpsimd", lambda eng: eng.dma_start(
        C_[0][:, :], dap(ATd, 0, [[NE * RK, 128], [128 * NE * RK, 16], [1, 512]])),
        [(S_["wF"], 16)])
    sem, v = item_consumed(("xr", 3))
    r.wait("gpsimd", sem, v)
    r.op("gpsimd", lambda eng: eng.dma_start(
        C_[1][:, :], dap(BTd, 0, [[D, 128], [128 * D, 4], [1, 2048]])), [(S_["bc"], 16)])

    ps_t_last = [None, None]
    mh_txes_after = {}
    tx_state = [0]
    e8_count_at = {}

    def x1T_block(mc, qt):
        base = (mc % 4) * 2048 + qt * 128
        return B_[mc // 4][:, base:base + 128]

    def emit_normalize(qt):
        r.wait("vector", S_["vch"], LN1_FIN1 if qt < 8 else LN1_FIN2)
        r.op("vector", lambda eng, qt=qt: eng.tensor_scalar(
            A_[qt][:, :], A_[qt][:, :], stcol[:, qt:qt + 1], stcol[:, 16 + qt:17 + qt],
            ALU.mult, ALU.add), [(S_["n1"], 1)])

    def emit_x1tx(qt):
        r.wait("tensor", S_["n1"], qt + 1)
        for mc in range(16):
            bank = tx_state[0] % 2
            if ps_t_last[bank] is not None:
                sem, cntv = ps_t_last[bank]
                r.wait("tensor", sem, cntv)
            r.op("tensor", lambda eng, qt=qt, mc=mc, bank=bank:
                 eng.transpose(psT[:, bank * 1024:bank * 1024 + 128],
                               A_[qt][:, mc * 128:(mc + 1) * 128], identb[:, :]),
                 [(S_["tx"], 1)])
            r.wait("vector", S_["tx"], S_["tx"].n)
            r.op("vector", lambda eng, qt=qt, mc=mc, bank=bank:
                 eng.tensor_copy(x1T_block(mc, qt), psT[:, bank * 1024:bank * 1024 + 128]),
                 [(S_["txev"], 1)])
            ps_t_last[bank] = (S_["txev"], S_["txev"].n)
            tx_state[0] += 1

    def emit_h_router(qt):
        hhalf = qt % 2
        r.wait("tensor", S_["txev"], 16 * (qt + 1))
        if qt == 0:
            r.wait("tensor", S_["wF"], 16)
            r.wait("tensor", S_["bc"], 16)
            r.wait("tensor", S_["ld0g"], 16)
        r.wait("tensor", S_["mk"], qt - 1)
        r.wait("tensor", S_["lg"], qt - 1)
        for mc in range(16):
            r.op("tensor", lambda eng, qt=qt, mc=mc, hhalf=hhalf:
                 eng.matmul(psD[:, hhalf * 512:(hhalf + 1) * 512],
                            x1T_block(mc, qt), C_[0][:, mc * 512:(mc + 1) * 512],
                            start=(mc == 0), stop=(mc == 15)), [])
            r.op("tensor", lambda eng, qt=qt, mc=mc, hhalf=hhalf:
                 eng.matmul(psA[:, hhalf * 512:hhalf * 512 + 8],
                            x1T_block(mc, qt), wrtb[:, mc * 8:(mc + 1) * 8],
                            start=(mc == 0), stop=(mc == 15)),
                 [(S_["h"], 1)] if mc == 15 else [])

    def emit_router_chain(qt):
        hhalf = qt % 2
        lg8 = lgts[:, (qt % 2) * 8:(qt % 2) * 8 + 8]
        r.wait("vector", S_["h"], qt + 1)
        if qt == 0:
            r.wait("vector", S_["ld0"], lds0)
        if qt >= 2:
            r.wait("vector", S_["sch"], e8_count_at[qt - 2])
        r.op("vector", lambda eng, qt=qt, hhalf=hhalf, lg8=lg8:
             eng.tensor_add(lg8, psA[:, hhalf * 512:hhalf * 512 + 8], brs[:, :]),
             [(S_["lg"], 1)])
        if qt >= 1:
            r.wait("vector", S_["sch"], e8_count_at[qt - 1])
        r.wait("vector", S_["lg"], S_["lg"].n)
        r.op("vector", lambda eng, lg8=lg8: eng.reduce_max(smf[:, 16:17], lg8, axis=AX.X,
                                                           negate=True), [(S_["vch"], 1)])
        r.op("vector", lambda eng, lg8=lg8: eng.reduce_max(smf[:, 17:18], lg8, axis=AX.X),
             [(S_["vch"], 1)])
        r.wait("scalar", S_["vch"], S_["vch"].n)
        r.op("scalar", lambda eng, lg8=lg8: eng.activation(smf[:, 0:8], lg8, AF.Exp,
                                                           bias=smf[:, 16:17],
                                                           accum_out=smf[:, 18:19]),
             [(S_["sch"], 1)])
        e8_count_at[qt] = S_["sch"].n
        r.wait("vector", S_["sch"], S_["sch"].n)
        r.op("vector", lambda eng: eng.reciprocal(smf[:, 19:20], smf[:, 18:19]),
             [(S_["vch"], 1)])
        r.op("vector", lambda eng, lg8=lg8: eng.tensor_scalar(smf[:, 8:16], lg8,
                                                              smf[:, 17:18], None, ALU.is_ge),
             [(S_["vch"], 1)])
        r.wait("vector", S_["vch"], S_["vch"].n)
        r.op("vector", lambda eng, qt=qt: eng.tensor_scalar(coefT[:, qt * 8:(qt + 1) * 8],
                                                            smf[:, 8:16], smf[:, 19:20],
                                                            None, ALU.mult), [(S_["co"], 1)])

    def emit_mask(qt):
        hhalf = qt % 2
        r.wait("vector", S_["co"], qt + 1)
        if qt >= 2:
            r.wait("vector", S_["txes"], mh_txes_after[qt - 2])
        for ex in range(NE):
            r.op("vector", lambda eng, qt=qt, ex=ex, hhalf=hhalf:
                 eng.tensor_scalar(mh[:, hhalf * 512 + ex * 64: hhalf * 512 + (ex + 1) * 64],
                                   psD[:, hhalf * 512 + ex * 64: hhalf * 512 + (ex + 1) * 64],
                                   coefT[:, qt * 8 + ex:qt * 8 + ex + 1], None, ALU.mult),
                 [(S_["mk"], 1)] if ex == NE - 1 else [])

    def emit_mhtx(qt):
        hhalf = qt % 2
        r.wait("tensor", S_["mk"], qt + 1)
        for blk in range(4):
            bank = tx_state[0] % 2
            if ps_t_last[bank] is not None:
                sem, cntv = ps_t_last[bank]
                r.wait("tensor", sem, cntv)
            r.op("tensor", lambda eng, qt=qt, blk=blk, bank=bank, hhalf=hhalf:
                 eng.transpose(psT[:, bank * 1024:bank * 1024 + 128],
                               mh[:, hhalf * 512 + blk * 128: hhalf * 512 + (blk + 1) * 128],
                               identb[:, :]), [(S_["tx"], 1)])
            if qt == 0 and blk == 0:
                r.wait("scalar", S_["r1"], 48)
            r.wait("scalar", S_["tx"], S_["tx"].n)
            r.op("scalar", lambda eng, qt=qt, blk=blk, bank=bank:
                 eng.copy(C_[2][:, blk * 2048 + qt * 128: blk * 2048 + (qt + 1) * 128],
                          psT[:, bank * 1024:bank * 1024 + 128]), [(S_["txes"], 1)])
            ps_t_last[bank] = (S_["txes"], S_["txes"].n)
            tx_state[0] += 1
        mh_txes_after[qt] = S_["txes"].n

    def emit_y(qt):
        r.wait("tensor", S_["txes"], S_["txes"].n)
        for oc in range(4):
            yidx = qt * 4 + oc
            ybank = yidx % 2
            r.wait("tensor", S_["x2"], yidx - 1)
            for pair in range(4):
                r.op("tensor", lambda eng, qt=qt, oc=oc, pair=pair, ybank=ybank:
                     eng.matmul(psE[:, ybank * 512:(ybank + 1) * 512],
                                C_[2][:, pair * 2048 + qt * 128: pair * 2048 + (qt + 1) * 128],
                                C_[1][:, pair * 2048 + oc * 512: pair * 2048 + (oc + 1) * 512],
                                start=(pair == 0), stop=(pair == 3)),
                     [(S_["yy"], 1)] if pair == 3 else [])
            r.wait("vector", S_["yy"], yidx + 1)
            r.op("vector", lambda eng, qt=qt, oc=oc, ybank=ybank:
                 eng.scalar_tensor_tensor(A_[qt][:, oc * 512:(oc + 1) * 512],
                                          psE[:, ybank * 512:(ybank + 1) * 512],
                                          1.0 / 64.0, A_[qt][:, oc * 512:(oc + 1) * 512],
                                          ALU.mult, ALU.add), [(S_["x2"], 1)])

    def emit_ln2_stats(qt):
        r.wait("vector", S_["x2"], 4 * (qt + 1))
        r.op("vector", lambda eng, qt=qt: eng.reduce_sum(stcol[:, 32 + qt:33 + qt],
                                                         A_[qt][:, :], axis=AX.X),
             [(S_["st1"], 1)])
        r.wait("scalar", S_["x2"], 4 * (qt + 1))
        r.wait("scalar", S_["sq1"], S_["sq1"].n)
        r.op("scalar", lambda eng, qt=qt: eng.activation(X[:, :], A_[qt][:, :],
                                                         AF.Square,
                                                         accum_out=stcol[:, 48 + qt:49 + qt]),
             [(S_["sq1"], 1)])

    def emit_p5(qt, fin):
        r.wait("vector", S_["vch"], fin)
        r.wait("vector", S_["out"], 16 * (qt - 1))
        r.op("vector", lambda eng, qt=qt: eng.tensor_scalar(
            obuf[qt % 2][:, :], A_[qt][:, :], stcol[:, 32 + qt:33 + qt],
            stcol[:, 48 + qt:49 + qt], ALU.mult, ALU.add), [(S_["n2"], 1)])
        r.wait("sync", S_["n2"], qt + 1)
        r.wait("sync", S_["out"], 16 * qt)
        r.op("sync", lambda eng, qt=qt: eng.dma_start(y.ap()[qt * 128:(qt + 1) * 128, :],
                                                      obuf[qt % 2][:, :]), [(S_["out"], 16)])

    for qt in range(16):
        emit_normalize(qt)
    emit_x1tx(0)
    LN2_FIN1 = None
    for qt in range(16):
        emit_h_router(qt)
        emit_router_chain(qt)
        emit_mask(qt)
        if qt + 1 < 16:
            emit_x1tx(qt + 1)
        emit_mhtx(qt)
        emit_y(qt)
        emit_ln2_stats(qt)
        if qt == 7:
            r.wait("vector", S_["st1"], 24)
            r.wait("vector", S_["sq1"], 24)
            ln_finalize(32, 48, 0, 16, 0, 8)
            LN2_FIN1 = S_["vch"].n
        if qt >= 8:
            emit_p5(qt - 8, LN2_FIN1)
    r.wait("vector", S_["st1"], 32)
    r.wait("vector", S_["sq1"], 32)
    ln_finalize(32, 48, 0, 16, 8, 8)
    LN2_FIN2 = S_["vch"].n
    for qt in range(8, 16):
        emit_p5(qt, LN2_FIN2)
    r.wait("sync", S_["out"], 16 * 16)

    r.check()

    with nc.Block() as block:
        @block.tensor
        def _(eng):
            for f in r.ops["tensor"]:
                f(eng)

        @block.scalar
        def _(eng):
            for f in r.ops["scalar"]:
                f(eng)

        @block.vector
        def _(eng):
            for f in r.ops["vector"]:
                f(eng)

        @block.sync
        def _(eng):
            for f in r.ops["sync"]:
                f(eng)

        @block.gpsimd
        def _(eng):
            for f in r.ops["gpsimd"]:
                f(eng)

    ctx.close()
    return nc


_NC_CACHE = None


def _get_nc():
    global _NC_CACHE
    if _NC_CACHE is None:
        _NC_CACHE = build_nc()
    return _NC_CACHE


def make_in_maps(L0_attn_x, L0_attn_wq, L0_attn_wk, L0_attn_wv, L0_attn_wo,
                 L0_ffn_A, L0_ffn_B, L0_ffn_Wr, L0_ffn_br):
    x = np.asarray(L0_attn_x, dtype=np.float32)
    shared = {
        "wqT": np.ascontiguousarray(np.asarray(L0_attn_wq, np.float32).T),
        "wkT": np.ascontiguousarray(np.asarray(L0_attn_wk, np.float32).T),
        "wvT": np.ascontiguousarray(np.asarray(L0_attn_wv, np.float32).T),
        "woT": np.ascontiguousarray(np.asarray(L0_attn_wo, np.float32).T),
        "ATd": np.ascontiguousarray(
            np.asarray(L0_ffn_A, np.float32).transpose(2, 0, 1).reshape(D, NE * RK)),
        "BTd": np.ascontiguousarray(
            np.asarray(L0_ffn_B, np.float32).transpose(0, 2, 1).reshape(4 * 128, D)),
        "WrTd": np.ascontiguousarray(np.asarray(L0_ffn_Wr, np.float32).T),
        "brb": np.ascontiguousarray(
            np.broadcast_to(np.asarray(L0_ffn_br, np.float32).reshape(1, NE), (128, NE))),
        "identd": np.eye(128, dtype=ml_dtypes.bfloat16),
        "onesmd": np.ones((128, 128), dtype=np.float32),
        "onesrd": np.ones((1, 128), dtype=ml_dtypes.bfloat16),
        "onesmbd": np.ones((128, 128), dtype=ml_dtypes.bfloat16),
    }
    in_maps = []
    for c in range(8):
        b, hf = c // 2, c % 2
        own = x[b, hf * NQ:(hf + 1) * NQ, :]
        m = dict(shared)
        m["xTb"] = np.ascontiguousarray(own.T)
        m["xnat"] = np.ascontiguousarray(own)
        in_maps.append(m)
    return in_maps


def kernel(**inputs):
    in_maps = make_in_maps(**inputs)
    nc = _get_nc()
    res = run_bass_kernel_spmd(nc, in_maps, core_ids=list(range(8)))
    out = np.empty((4, 4096, 2048), dtype=np.float32)
    for c in range(8):
        b, hf = c // 2, c % 2
        out[b, hf * NQ:(hf + 1) * NQ, :] = res.results[c]["y"]
    return out


# version 17 (final)
# speedup vs baseline: 1.0764x; 1.0387x over previous
import numpy as np
import ml_dtypes
import concourse.bass as bass
import concourse.mybir as mybir
from concourse.bass_utils import run_bass_kernel_spmd

F32 = mybir.dt.float32
BF16 = mybir.dt.bfloat16
AF = mybir.ActivationFunctionType
ALU = mybir.AluOpType
AX = mybir.AxisListType

D = 2048
S = 4096
NQ = 2048
H = 16
HD = 128
NE = 8
RK = 64
BETA = 9.5
EPS = 1e-5

ENGS = ("tensor", "scalar", "vector", "sync", "gpsimd")


class Sem:
    def __init__(self, h, name):
        self.h = h
        self.name = name
        self.n = 0


class Rec:
    def __init__(self):
        self.ops = {e: [] for e in ENGS}
        self.waits = []

    def wait(self, e, sem, val):
        if val <= 0:
            return
        self.waits.append((sem, val))
        self.ops[e].append(lambda eng, s=sem, v=val: eng.wait_ge(s.h, v))

    def op(self, e, build, incs=()):
        incs = tuple(incs)
        for s, a in incs:
            s.n += a
        assert len(incs) <= 1, "one sem update per instruction"
        attach, spill = incs[:1], incs[1:]

        def f(eng, build=build, attach=attach, spill=spill):
            inst = build(eng)
            for s, a in attach:
                inst = inst.then_inc(s.h, a)
            for s, a in spill:
                eng.sem_inc(s.h, a)
        self.ops[e].append(f)

    def check(self):
        for s, v in self.waits:
            assert v <= s.n, f"wait on {s.name} for {v} but total incs {s.n}"


def dap(t, off, dims):
    return bass.AP(t, off, [list(d) for d in dims])


def build_nc():
    nc = bass.Bass("TRN2", target_bir_lowering=False)

    xTb = nc.dram_tensor("xTb", [D, NQ], F32, kind="ExternalInput")
    xnat = nc.dram_tensor("xnat", [NQ, D], F32, kind="ExternalInput")
    wqT = nc.dram_tensor("wqT", [D, D], F32, kind="ExternalInput")
    wkT = nc.dram_tensor("wkT", [D, D], F32, kind="ExternalInput")
    wvT = nc.dram_tensor("wvT", [D, D], F32, kind="ExternalInput")
    woT = nc.dram_tensor("woT", [D, D], F32, kind="ExternalInput")
    ATd = nc.dram_tensor("ATd", [D, NE * RK], F32, kind="ExternalInput")
    BTd = nc.dram_tensor("BTd", [4 * 128, D], F32, kind="ExternalInput")
    WrTd = nc.dram_tensor("WrTd", [D, NE], F32, kind="ExternalInput")
    brb = nc.dram_tensor("brb", [128, NE], F32, kind="ExternalInput")
    identd = nc.dram_tensor("identd", [128, 128], BF16, kind="ExternalInput")
    onesmd = nc.dram_tensor("onesmd", [128, 128], F32, kind="ExternalInput")
    onesmbd = nc.dram_tensor("onesmbd", [128, 128], BF16, kind="ExternalInput")
    onesrd = nc.dram_tensor("onesrd", [1, 128], BF16, kind="ExternalInput")

    y = nc.dram_tensor("y", [NQ, D], F32, kind="ExternalOutput")

    kdram = nc.dram_tensor("kdram", [8, 128, 8192], BF16)
    kown = nc.dram_tensor("kown", [4, 128, 8192], BF16)
    vown = nc.dram_tensor("vown", [NQ, D], BF16)
    vdram = nc.dram_tensor("vdram", [S, D], BF16)
    qdram = nc.dram_tensor("qdram", [4, 128, 8192], BF16)
    negdram = nc.dram_tensor("negdram", [16, 2048], BF16)

    from contextlib import ExitStack
    ctx = ExitStack()
    e = ctx.enter_context

    AB = [e(nc.sbuf_tensor(f"AB{i}", [128, 8192], BF16)) for i in range(4)]
    A_ = [AB[i // 4][:, (i % 4) * 2048:(i % 4 + 1) * 2048] for i in range(16)]
    B_ = [e(nc.sbuf_tensor(f"B{i}", [128, 8192], BF16)) for i in range(4)]
    C_ = [e(nc.sbuf_tensor(f"C{i}", [128, 8192], BF16)) for i in range(3)]
    obuf = [e(nc.sbuf_tensor(f"obuf{i}", [128, 2048], F32)) for i in range(2)]
    mh = e(nc.sbuf_tensor("mh", [128, 1024], BF16))
    coefT = e(nc.sbuf_tensor("coefT", [128, 128], F32))
    stcol = e(nc.sbuf_tensor("stcol", [128, 64], F32))
    smf = e(nc.sbuf_tensor("smf", [128, 32], F32))
    lgts = e(nc.sbuf_tensor("lgts", [128, 16], F32))
    wrtb = e(nc.sbuf_tensor("wrtb", [128, 128], BF16))
    brs = e(nc.sbuf_tensor("brs", [128, NE], F32))
    identb = e(nc.sbuf_tensor("identb", [128, 128], BF16))
    onesm = e(nc.sbuf_tensor("onesm", [128, 128], F32))
    onesr = e(nc.sbuf_tensor("onesr", [1, 128], BF16))
    onesmb = e(nc.sbuf_tensor("onesmb", [128, 128], BF16))
    acc2 = e(nc.sbuf_tensor("acc2", [128, 2048], BF16))
    pa = e(nc.sbuf_tensor("pa", [128, 32], F32))
    pab = e(nc.sbuf_tensor("pab", [128, 16], BF16))
    X = e(nc.sbuf_tensor("X", [128, 2048], BF16))

    psA = e(nc.psum_tensor("psA", [128, 1024], F32))
    psB_cm = nc.psum_tensor("psB", [128, 1024], F32)
    psB = psB_cm.__enter__()
    psC_cm = nc.psum_tensor("psC", [128, 1024], F32)
    psC = psC_cm.__enter__()
    psF_cm = nc.psum_tensor("psF", [128, 1024], F32)
    psF = psF_cm.__enter__()

    semnames = ("wA wAf x0 x1 xf0 xf1 mmP1 evP1 sp kv0 kv1 kvg0 kvg1 sc exp avf "
                "bc nrm c0 c1 c2 op r1 st1 sq1 vch sch n1 tx "
                "ccs asc ard stq xrg xr csd dnb rcp lnr xg0 xg1 xg2 xg3 xg4 xg5 xg6 xg7 "
                "txev txes wF h lg co mk yy x2 n2 out ld0 ld0g").split()
    S_ = {n: Sem(e(nc.semaphore(f"s_{n}")), n) for n in semnames}

    r = Rec()

    for dst, src in ((identb, identd), (onesm, onesmd), (onesmb, onesmbd), (onesr, onesrd), (brs, brb)):
        r.wait("sync", S_["ld0"], S_["ld0"].n)
        r.op("sync", lambda eng, d=dst, s=src: eng.dma_start(d[:, :], s.ap()[:, :]),
             [(S_["ld0"], 16)])
    r.op("gpsimd", lambda eng: eng.dma_start(
        wrtb[:, :], dap(WrTd, 0, [[NE, 128], [128 * NE, 16], [1, NE]])), [(S_["ld0g"], 16)])
    lds0 = S_["ld0"].n

    def pool6(i):
        i = i % 6
        if i < 2:
            return psB[:, i * 512:(i + 1) * 512]
        if i < 4:
            return psC[:, (i - 2) * 512:(i - 2 + 1) * 512]
        return psF[:, (i - 4) * 512:(i - 4 + 1) * 512]

    g_grp = 0
    n_spills = 0
    vslot_i = 0
    stage_last_spill = {}
    xcnk_free_at = {}
    pass_end_mm = []

    passes = (("k", wkT, 4, kown), ("v", wvT, 4, vown), ("q", wqT, 4, qdram))
    x_after = {}

    def emit_xchunk(t):
        par = t % 2
        if par in xcnk_free_at:
            r.wait("gpsimd", S_["mmP1"], xcnk_free_at[par])
        r.op("gpsimd", lambda eng, par=par, t=t: eng.dma_start(
            C_[par][:, :], dap(xTb, t * 512, [[NQ, 128], [128 * NQ, 16], [1, 512]])),
            [(S_["x" + str(par)], 16)])
        x_after[t] = S_["x" + str(par)].n

    for p_i, (kind, wT, nchunks, dest) in enumerate(passes):
        emit_xchunk(0)
        w_after = {}
        for g4 in range(4):
            wsem = S_["wA"] if g4 % 2 == 0 else S_["wAf"]
            if p_i > 0:
                r.wait("gpsimd", S_["mmP1"],
                       pass_end_mm[p_i - 1] - 16 + 4 * (g4 + 1))
            r.wait("gpsimd", wsem, wsem.n)
            r.op("gpsimd", lambda eng, g4=g4, wT=wT: eng.dma_start(
                AB[g4][:, :],
                dap(wT, g4 * 512, [[D, 128], [128 * D, 16], [1, 512]])),
                [(wsem, 16)])
            w_after[g4] = (wsem, wsem.n)

        for t in range(nchunks):
            par = t % 2
            if t + 1 < nchunks:
                emit_xchunk(t + 1)
            if kind == "v" and t == 1:
                r.wait("gpsimd", S_["sp"], n_spills * 16)
                r.op("gpsimd", lambda eng: eng.collective_compute(
                    "AllGather", ALU.bypass,
                    replica_groups=[[0, 1], [2, 3], [4, 5], [6, 7]],
                    ins=[kown.ap().opt()], outs=[kdram.ap().opt()]),
                    [(S_["ccs"], 1)])
            if kind == "q" and t == 1:
                r.wait("gpsimd", S_["sp"], n_spills * 16)
                r.op("gpsimd", lambda eng: eng.collective_compute(
                    "AllGather", ALU.bypass,
                    replica_groups=[[0, 1], [2, 3], [4, 5], [6, 7]],
                    ins=[vown.ap().opt()], outs=[vdram.ap().opt()]),
                    [(S_["ccs"], 1)])
            r.wait("tensor", S_["x" + str(par)], x_after[t])

            if kind in ("k", "q"):
                for dk in range(16):
                    if dk % 4 == 0:
                        r.wait("tensor", *w_after[dk // 4])
                    slot = g_grp % 6
                    r.wait("tensor", S_["evP1"], g_grp - 5)
                    for mc in range(16):
                        incs = [(S_["mmP1"], 1)] if mc == 15 else []
                        r.op("tensor", lambda eng, slot=slot, mc=mc, dk=dk, par=par:
                             eng.matmul(pool6(slot),
                                        AB[dk // 4][:, mc * 512 + (dk % 4) * 128:
                                                    mc * 512 + (dk % 4) * 128 + 128],
                                        C_[par][:, mc * 512:(mc + 1) * 512],
                                        start=(mc == 0), stop=(mc == 15)), incs)
                    r.wait("scalar", S_["mmP1"], S_["mmP1"].n)
                    if dk == 0 and ("kq", par) in stage_last_spill:
                        r.wait("scalar", S_["sp"], 16 * stage_last_spill[("kq", par)])
                    r.op("scalar", lambda eng, slot=slot, dk=dk, par=par:
                         eng.copy(B_[par][:, dk * 512:(dk + 1) * 512], pool6(slot)),
                         [(S_["evP1"], 1)])
                    g_grp += 1
                r.wait("sync", S_["evP1"], S_["evP1"].n)
                r.wait("sync", S_["sp"], S_["sp"].n)
                r.op("sync", lambda eng, t=t, par=par, dest=dest:
                     eng.dma_start(dest.ap()[t, :, :], B_[par][:, :]), [(S_["sp"], 16)])
                n_spills += 1
                stage_last_spill[("kq", par)] = n_spills
            else:
                for dvc in range(4):
                    r.wait("tensor", *w_after[dvc])
                    for tt in range(4):
                        vs = (t % 2) * 4 + tt
                        tt_global = t * 4 + tt
                        slot = g_grp % 6
                        r.wait("tensor", S_["evP1"], g_grp - 5)
                        for mc in range(16):
                            incs = [(S_["mmP1"], 1)] if mc == 15 else []
                            r.op("tensor", lambda eng, slot=slot, mc=mc, tt=tt,
                                 dvc=dvc, par=par:
                                 eng.matmul(pool6(slot),
                                            C_[par][:, mc * 512 + tt * 128:
                                                    mc * 512 + (tt + 1) * 128],
                                            AB[dvc][:, mc * 512:(mc + 1) * 512],
                                            start=(mc == 0), stop=(mc == 15)), incs)
                        r.wait("scalar", S_["mmP1"], S_["mmP1"].n)
                        if dvc == 0 and ("v", vs) in stage_last_spill:
                            r.wait("scalar", S_["sp"], 16 * stage_last_spill[("v", vs)])
                        r.op("scalar", lambda eng, slot=slot, vs=vs, dvc=dvc:
                             eng.copy(B_[2 + vs // 4][:, (vs % 4) * 2048 + dvc * 512:
                                                      (vs % 4) * 2048 + (dvc + 1) * 512],
                                      pool6(slot)), [(S_["evP1"], 1)])
                        g_grp += 1
                        if dvc == 3:
                            r.wait("sync", S_["evP1"], S_["evP1"].n)
                            r.wait("sync", S_["sp"], S_["sp"].n)
                            r.op("sync", lambda eng, tt_global=tt_global, vs=vs:
                                 eng.dma_start(
                                     vown.ap()[tt_global * 128:(tt_global + 1) * 128, :],
                                     B_[2 + vs // 4][:, (vs % 4) * 2048:(vs % 4 + 1) * 2048]),
                                 [(S_["sp"], 16)])
                            n_spills += 1
                            stage_last_spill[("v", vs)] = n_spills
            xcnk_free_at[par] = S_["mmP1"].n
        pass_end_mm.append(S_["mmP1"].n)

    P1_SPILLS = S_["sp"].n
    P1_MM = S_["mmP1"].n
    EV_TOT = S_["evP1"].n

    def ktbuf(par, piece):
        return A_[2 * par + piece]

    def vbuf(par, piece):
        return A_[4 + 2 * par + piece]

    def qbuf(par):
        return A_[8 + par]

    def ring(i):
        return A_[10 + (i % 10) // 2][:, ((i % 10) % 2) * 1024:(((i % 10) % 2) + 1) * 1024]

    def attnT(vh, lo, width):
        hh, qh = vh // 2, vh % 2
        base = (hh % 4) * 2048 + qh * 1024 + lo
        return B_[hh // 4][:, base:base + width]

    def chslot(vh):
        return acc2[:, (vh % 2) * 1024:(vh % 2 + 1) * 1024]

    def pslot(vh):
        return acc2[:, ((vh + 1) % 2) * 1024:((vh + 1) % 2 + 1) * 1024]

    NVH = 32
    kv_snap = {}
    kvv_snap = {}

    def emit_loads(hh):
        par = hh % 2
        if hh < 2:
            r.wait("sync", S_["sp"], P1_SPILLS)
            r.wait("sync", S_["mmP1"], P1_MM)
            r.wait("sync", S_["ccs"], 2)
        else:
            r.wait("sync", S_["sc"], 64 * (hh - 1))
            r.wait("sync", S_["avf"], 64 * (hh - 1))
        ks = S_["kv" + str(par)]
        for piece in range(2):
            r.wait("sync", ks, ks.n)
            r.op("sync", lambda eng, hh=hh, g=piece, par=par: eng.dma_start(
                ktbuf(par, g)[:, :],
                dap(kdram, (4 * g) * 128 * 8192 + hh * 512,
                    [[8192, 128], [128 * 8192, 4], [1, 512]])), [(ks, 16)])
        r.wait("sync", ks, ks.n)
        r.op("sync", lambda eng, hh=hh, par=par: eng.dma_start(
            qbuf(par)[:, :],
            dap(qdram, hh * 512, [[8192, 128], [128 * 8192, 4], [1, 512]])),
            [(ks, 16)])
        kv_snap[hh] = ks.n
        for piece in range(2):
            r.wait("sync", ks, ks.n)
            r.op("sync", lambda eng, hh=hh, g=piece, par=par: eng.dma_start(
                vbuf(par, g)[:, :],
                dap(vdram, (g * 16 * 128) * D + hh * 128,
                    [[D, 128], [128 * D, 16], [1, 128]])), [(ks, 16)])
        kvv_snap[hh] = ks.n

    turns = []
    turn_of_kt = {}
    turn_of_den = {}
    for vh in range(NVH):
        for kt in range(32):
            if kt == 4 and vh >= 1:
                turn_of_den[vh - 1] = len(turns)
                turns.append(("den", vh - 1))
            turn_of_kt[(vh, kt)] = len(turns)
            turns.append(("kt", vh, kt))
    turn_of_den[31] = len(turns)
    turns.append(("den", 31))

    def turn_slot(T):
        return psA if T % 2 == 0 else psC

    def turn_consumed(T):
        t = turns[T]
        if t[0] == "kt":
            return (S_["exp"], t[1] * 32 + t[2] + 1)
        return (S_["lnr"], t[1] + 1)

    def emit_passA_mm(ga):
        vh, a = ga // 64, ga % 64
        hh, qh, par = vh // 2, vh % 2, (vh // 2) % 2
        j, c = a // 8, a % 8
        if a == 0:
            r.wait("tensor", S_["kv" + str(par)], kv_snap[hh])
        if ga == 0:
            r.wait("tensor", S_["evP1"], EV_TOT)
        r.wait("tensor", S_["ard"], ga - 1)
        r.op("tensor", lambda eng, ga=ga, qh=qh, par=par, j=j, c=c: eng.matmul(
            psF[:, (ga % 2) * 512:(ga % 2 + 1) * 512],
            qbuf(par)[:, qh * 1024 + j * 128:qh * 1024 + (j + 1) * 128],
            ktbuf(par, c // 4)[:, (c % 4) * 512:(c % 4 + 1) * 512],
            start=True, stop=True), [(S_["asc"], 1)])

    def emit_passA_dve(ga):
        vh, a = ga // 64, ga % 64
        j, c = a // 8, a % 8
        scc = (j % 2) * 8 + c
        r.wait("vector", S_["asc"], ga + 1)
        r.op("vector", lambda eng, ga=ga, scc=scc: eng.reduce_max(
            pa[:, scc:scc + 1], psF[:, (ga % 2) * 512:(ga % 2 + 1) * 512],
            axis=AX.X), [(S_["ard"], 1)])
        if c == 7:
            stc = (vh % 2) * 8 + j
            r.wait("vector", S_["ard"], S_["ard"].n)
            if vh >= 2:
                r.wait("vector", S_["xg" + str(j)], 16 * (vh - 1))
            r.op("vector", lambda eng, j=j, stc=stc: eng.reduce_max(
                pab[:, stc:stc + 1], pa[:, (j % 2) * 8:(j % 2) * 8 + 8],
                axis=AX.X, negate=True), [(S_["stq"], 1)])

    def emit_stat_dma(vh, j):
        stc = (vh % 2) * 8 + j
        qh = vh % 2
        xg = S_["xg" + str(j)]
        if vh == 0 and j == 0:
            r.wait("gpsimd", S_["sp"], P1_SPILLS)
        r.wait("gpsimd", xg, 16 * vh)
        if vh >= 2 and j == 0:
            r.wait("gpsimd", S_["sc"], 32 * (vh - 1))
        r.wait("gpsimd", S_["stq"], 8 * vh + j + 1)
        r.op("gpsimd", lambda eng, j=j, qh=qh, stc=stc: eng.dma_start(
            X[32 * qh:32 * qh + 1, qh * 1024 + j * 128:qh * 1024 + (j + 1) * 128],
            pab[:, stc:stc + 1]), [(xg, 16)])

    def casc_ops(vh, t):
        CH, P = (lambda: chslot(vh)), (lambda: pslot(vh))
        R = lambda k: (lambda: ring(32 * vh + k))
        if t == 5:
            return [(CH, R(0), R(1), [0, 1])]
        if 6 <= t <= 11:
            return [(CH, CH, R(t - 4), [t - 4])]
        if t == 12:
            return [(P, R(8), R(9), [8, 9])]
        if 13 <= t <= 18:
            return [(P, P, R(t - 3), [t - 3])]
        if t == 19:
            return [(P, P, CH, [])]
        if t == 20:
            return [(CH, R(16), R(17), [16, 17])]
        if 21 <= t <= 26:
            return [(CH, CH, R(t - 3), [t - 3])]
        if t == 27:
            return [(P, P, CH, [])]
        if t == 28:
            return [(CH, R(24), R(25), [24, 25])]
        if 29 <= t <= 33:
            return [(CH, CH, R(t - 3), [t - 3])]
        if t == 34:
            return [(CH, CH, R(31), [31])]
        if t == 35:
            return [(P, P, CH, [])]
        return []

    csd_at = {}
    csd_final = {}

    def emit_cascade(vh, t):
        for dst, s0, s1, cons in casc_ops(vh, t):
            if cons:
                r.wait("vector", S_["exp"], 32 * vh + max(cons) + 1)
            if t == 5:
                r.wait("vector", S_["dnb"], 2 * vh)
            r.wait("vector", S_["csd"], S_["csd"].n)
            r.op("vector", lambda eng, dst=dst, s0=s0, s1=s1: eng.tensor_add(
                dst(), s0(), s1()), [(S_["csd"], 1)])
            for k in cons:
                csd_at[32 * vh + k] = S_["csd"].n
            if t == 35:
                csd_final[vh] = S_["csd"].n

    def emit_den(vhd):
        T = turn_of_den[vhd]
        slot = turn_slot(T)
        if T >= 2:
            sem, v = turn_consumed(T - 2)
            r.wait("tensor", sem, v)
        r.wait("tensor", S_["exp"], 32 * (vhd + 1))
        r.wait("tensor", S_["csd"], csd_final[vhd])
        for qc in range(2):
            r.op("tensor", lambda eng, qc=qc, vhd=vhd, slot=slot: eng.matmul(
                slot[:, qc * 512:(qc + 1) * 512], onesmb[:, :],
                pslot(vhd)[:, qc * 512:(qc + 1) * 512],
                start=True, stop=True), [(S_["dnb"], 1)])

    def emit_recip(vhd):
        slot = turn_slot(turn_of_den[vhd])
        r.wait("scalar", S_["dnb"], 2 * (vhd + 1))
        r.wait("scalar", S_["rcp"], vhd - 1)
        r.op("scalar", lambda eng, vhd=vhd, slot=slot: eng.activation(
            obuf[0][:, (vhd % 2) * 1024:(vhd % 2 + 1) * 1024],
            slot[:, 0:1024], AF.Ln), [(S_["lnr"], 1)])
        r.wait("scalar", S_["lnr"], S_["lnr"].n)
        r.op("scalar", lambda eng, vhd=vhd: eng.activation(
            obuf[1][:, (vhd % 2) * 1024:(vhd % 2 + 1) * 1024],
            obuf[0][:, (vhd % 2) * 1024:(vhd % 2 + 1) * 1024],
            AF.Exp, scale=np.float32(-1.0)), [(S_["rcp"], 1)])

    def emit_evac(vhd, h):
        r.wait("vector", S_["rcp"], vhd + 1)
        r.wait("vector", S_["avf"], 32 * (vhd + 1))
        if vhd == 0 and h == 0:
            r.wait("vector", S_["sp"], P1_SPILLS)
        r.op("vector", lambda eng, h=h, vhd=vhd: eng.tensor_mul(
            attnT(vhd, h * 512, 512), psB[:, h * 512:(h + 1) * 512],
            obuf[1][:, (vhd % 2) * 1024 + h * 512:(vhd % 2) * 1024 + (h + 1) * 512]),
            [(S_["nrm"], 1)])

    def emit_expB(vh, kt):
        gk = 32 * vh + kt
        T = turn_of_kt[(vh, kt)]
        slot = turn_slot(T)
        r.wait("scalar", S_["sc"], gk + 1)
        if gk >= 10:
            r.wait("scalar", S_["avf"], gk - 9)
            r.wait("scalar", S_["csd"], csd_at[gk - 10])
        r.op("scalar", lambda eng, gk=gk, slot=slot: eng.activation(
            ring(gk), slot[:, :], AF.Exp), [(S_["exp"], 1)])

    def emit_scoresB(vh, kt):
        gk = 32 * vh + kt
        T = turn_of_kt[(vh, kt)]
        slot = turn_slot(T)
        qh, par = vh % 2, (vh // 2) % 2
        if kt == 0:
            for j in range(8):
                r.wait("tensor", S_["xg" + str(j)], 16 * (vh + 1))
        if T >= 2:
            sem, v = turn_consumed(T - 2)
            r.wait("tensor", sem, v)
        piece, tile = kt // 16, kt % 16
        for qc in range(2):
            r.op("tensor", lambda eng, qc=qc, qh=qh, slot=slot: eng.matmul(
                slot[:, qc * 512:(qc + 1) * 512], onesmb[32 * qh:32 * qh + 1, :],
                X[32 * qh:32 * qh + 1, qh * 1024 + qc * 512:qh * 1024 + (qc + 1) * 512],
                start=True, stop=False), [])
        for qc in range(2):
            r.op("tensor", lambda eng, piece=piece, tile=tile, qc=qc, par=par, qh=qh,
                 slot=slot:
                 eng.matmul(slot[:, qc * 512:(qc + 1) * 512],
                            ktbuf(par, piece)[:, tile * 128:(tile + 1) * 128],
                            qbuf(par)[:, qh * 1024 + qc * 512:qh * 1024 + (qc + 1) * 512],
                            start=False, stop=True),
                 [(S_["sc"], 1)] if qc == 1 else [])

    av_q = []
    for vhV in range(NVH):
        for ktV in range(32):
            av_q.append((vhV * 32 + max(9, ktV + 2), vhV, ktV))
    av_q.sort()

    def emit_av_pops(p, limit=2):
        n_av = 0
        while av_q and av_q[0][0] <= p and n_av < limit:
            _, vhV, ktV = av_q.pop(0)
            parV = (vhV // 2) % 2
            gkV = vhV * 32 + ktV
            if ktV == 0:
                r.wait("tensor", S_["kv" + str(parV)], kvv_snap[vhV // 2])
            r.wait("tensor", S_["exp"], gkV + 1)
            piece, tile = ktV // 16, ktV % 16
            for qc in range(2):
                if ktV == 0:
                    r.wait("tensor", S_["nrm"], 2 * (vhV - 1) + qc + 1)
                r.op("tensor", lambda eng, piece=piece, tile=tile, qc=qc, parV=parV,
                     gkV=gkV, ktV=ktV:
                     eng.matmul(psB[:, qc * 512:(qc + 1) * 512],
                                vbuf(parV, piece)[:, tile * 128:(tile + 1) * 128],
                                ring(gkV)[:, qc * 512:(qc + 1) * 512],
                                start=(ktV == 0), stop=(ktV == 31)),
                     [(S_["avf"], 1)] if qc == 1 else [])
            n_av += 1

    emit_loads(0)
    emit_loads(1)
    r.wait("tensor", S_["ld0"], lds0)
    for ga in range(96):
        emit_passA_mm(ga)
        emit_passA_dve(ga)
        if ga == 63:
            for j in range(8):
                emit_stat_dma(0, j)
    for j in range(4):
        emit_stat_dma(1, j)

    NPOS = NVH * 32
    for p in range(NPOS):
        vh, kt = p // 32, p % 32

        for hh in range(2, 16):
            if p == 64 * (hh - 1):
                emit_loads(hh)
        for j in range(4, 8):
            if p == 4 * (j - 4) + 4:
                emit_stat_dma(1, j)
        for vhg in range(2, NVH):
            for j in range(8):
                if p == 32 * (vhg - 2) + 20 + 4 * j:
                    emit_stat_dma(vhg, j)

        if kt == 4 and vh >= 1:
            emit_den(vh - 1)
        emit_scoresB(vh, kt)
        ga0 = 2 * (p + 48)
        if ga0 + 1 < 2048:
            emit_passA_mm(ga0)
            emit_passA_mm(ga0 + 1)
        emit_av_pops(p)

        if kt == 5 and vh >= 1:
            emit_recip(vh - 1)
        emit_expB(vh, kt)

        if kt in (0, 1, 2, 3) and vh >= 1:
            emit_cascade(vh - 1, 32 + kt)
        if kt == 6 and vh >= 1:
            emit_evac(vh - 1, 0)
        if kt == 7 and vh >= 1:
            emit_evac(vh - 1, 1)
        if ga0 + 1 < 2048:
            emit_passA_dve(ga0)
            emit_passA_dve(ga0 + 1)
        emit_cascade(vh, kt)

    emit_av_pops(10 ** 9, limit=64)
    for t in (32, 33, 34, 35):
        emit_cascade(31, t)
    emit_den(31)
    emit_recip(31)
    emit_evac(31, 0)
    emit_evac(31, 1)
    NRM_TOTAL = S_["nrm"].n

    psF_cm.__exit__(None, None, None)
    psC_cm.__exit__(None, None, None)
    psB_cm.__exit__(None, None, None)
    psD = e(nc.psum_tensor("psD", [128, 1024], F32))
    psT = e(nc.psum_tensor("psT", [128, 2048], BF16))
    psE = e(nc.psum_tensor("psE", [128, 1024], F32))

    slot_items = []
    for g in range(4):
        slot_items.append(("wo", g))
        slot_items.append(("xr", g))
    slot_of = {}
    slot_uses = [0, 0, 0]
    for i, item in enumerate(slot_items):
        slot_of[item] = i % 3

    def item_consumed(item):
        knd, g = item
        return (S_["op"], 16 * (g + 1)) if knd == "wo" else (S_["r1"], 16 * (g + 1))

    for i, (knd, g) in enumerate(slot_items):
        slot = i % 3
        if slot_uses[slot] == 0:
            r.wait("gpsimd", S_["mmP1"], P1_MM)
        else:
            sem, v = item_consumed(slot_items[i - 3])
            r.wait("gpsimd", sem, v)
        src = woT if knd == "wo" else xnat
        r.op("gpsimd", lambda eng, slot=slot, g=g, src=src: eng.dma_start(
            C_[slot][:, :], dap(src, g * 512, [[D, 128], [128 * D, 16], [1, 512]])),
            [(S_["c" + str(slot)], 16)])
        slot_uses[slot] += 1
    c_counts = {}
    cnt = [0, 0, 0]
    for i, item in enumerate(slot_items):
        slot = i % 3
        cnt[slot] += 16
        c_counts[item] = cnt[slot]

    def ln_finalize(r_s1, r_s2, r_a, r_b, lo=0, n=16):
        def rng(base):
            return stcol[:, base + lo:base + lo + n]
        r.op("vector", lambda eng: eng.tensor_scalar(rng(r_a), rng(r_s1),
                                                     float(1.0 / D), None, ALU.mult),
             [(S_["vch"], 1)])
        r.op("vector", lambda eng: eng.tensor_scalar(rng(r_b), rng(r_s2),
                                                     float(1.0 / D), None, ALU.mult),
             [(S_["vch"], 1)])
        r.wait("vector", S_["vch"], S_["vch"].n)
        r.op("vector", lambda eng: eng.tensor_mul(rng(r_s1), rng(r_a), rng(r_a)),
             [(S_["vch"], 1)])
        r.wait("vector", S_["vch"], S_["vch"].n)
        r.op("vector", lambda eng: eng.tensor_sub(rng(r_s2), rng(r_b), rng(r_s1)),
             [(S_["vch"], 1)])
        r.wait("scalar", S_["vch"], S_["vch"].n)
        r.op("scalar", lambda eng: eng.activation(rng(r_b), rng(r_s2), AF.Sqrt,
                                                  bias=np.float32(EPS)), [(S_["sch"], 1)])
        r.wait("vector", S_["sch"], S_["sch"].n)
        r.op("vector", lambda eng: eng.reciprocal(rng(r_s1), rng(r_b)), [(S_["vch"], 1)])
        r.wait("vector", S_["vch"], S_["vch"].n)
        r.op("vector", lambda eng: eng.tensor_mul(rng(r_b), rng(r_a), rng(r_s1)),
             [(S_["vch"], 1)])
        r.wait("vector", S_["vch"], S_["vch"].n)
        r.op("vector", lambda eng: eng.tensor_scalar(rng(r_s2), rng(r_b),
                                                     -1.0, None, ALU.mult),
             [(S_["vch"], 1)])

    r.wait("tensor", S_["nrm"], NRM_TOTAL)
    opidx = 0
    LN1_FIN1 = LN1_FIN2 = None
    for g in range(4):
        swo = slot_of[("wo", g)]
        sxr = slot_of[("xr", g)]
        r.wait("tensor", S_["c" + str(swo)], c_counts[("wo", g)])
        for qt in range(16):
            half = opidx % 2
            r.wait("tensor", S_["r1"], opidx - 1)
            for dc in range(16):
                incs = [(S_["op"], 1)] if dc == 15 else []
                r.op("tensor", lambda eng, g=g, qt=qt, dc=dc, half=half, swo=swo:
                     eng.matmul(psD[:, half * 512:(half + 1) * 512],
                                B_[dc // 4][:, (dc % 4) * 2048 + qt * 128:(dc % 4) * 2048 + (qt + 1) * 128],
                                C_[swo][:, dc * 512:(dc + 1) * 512],
                                start=(dc == 0), stop=(dc == 15)), incs)
            r.wait("vector", S_["op"], opidx + 1)
            if qt == 0:
                r.wait("vector", S_["c" + str(sxr)], c_counts[("xr", g)])
            r.op("vector", lambda eng, g=g, qt=qt, half=half, sxr=sxr:
                 eng.tensor_add(A_[qt][:, g * 512:(g + 1) * 512],
                                psD[:, half * 512:(half + 1) * 512],
                                C_[sxr][:, qt * 512:(qt + 1) * 512]), [(S_["r1"], 1)])
            opidx += 1
            if g == 3:
                r.wait("vector", S_["r1"], 48 + qt + 1)
                r.op("vector", lambda eng, qt=qt: eng.reduce_sum(
                    stcol[:, qt:qt + 1], A_[qt][:, :], axis=AX.X), [(S_["st1"], 1)])
                r.wait("scalar", S_["r1"], 48 + qt + 1)
                r.wait("scalar", S_["sq1"], S_["sq1"].n)
                r.op("scalar", lambda eng, qt=qt: eng.activation(
                    X[:, :], A_[qt][:, :], AF.Square,
                    accum_out=stcol[:, 16 + qt:17 + qt]), [(S_["sq1"], 1)])
                if qt == 7:
                    r.wait("vector", S_["st1"], 8)
                    r.wait("vector", S_["sq1"], 8)
                    ln_finalize(0, 16, 32, 48, 0, 8)
                    LN1_FIN1 = S_["vch"].n
                if qt == 15:
                    r.wait("vector", S_["st1"], 16)
                    r.wait("vector", S_["sq1"], 16)
                    ln_finalize(0, 16, 32, 48, 8, 8)
                    LN1_FIN2 = S_["vch"].n

    sem, v = item_consumed(("wo", 3))
    r.wait("gpsimd", sem, v)
    r.op("gpsimd", lambda eng: eng.dma_start(
        C_[0][:, :], dap(ATd, 0, [[NE * RK, 128], [128 * NE * RK, 16], [1, 512]])),
        [(S_["wF"], 16)])
    sem, v = item_consumed(("xr", 3))
    r.wait("gpsimd", sem, v)
    r.op("gpsimd", lambda eng: eng.dma_start(
        C_[1][:, :], dap(BTd, 0, [[D, 128], [128 * D, 4], [1, 2048]])), [(S_["bc"], 16)])

    ps_t_last = [None, None]
    mh_txes_after = {}
    tx_state = [0]
    e8_count_at = {}

    def x1T_block(mc, qt):
        base = (mc % 4) * 2048 + qt * 128
        return B_[mc // 4][:, base:base + 128]

    def emit_normalize(qt):
        r.wait("vector", S_["vch"], LN1_FIN1 if qt < 8 else LN1_FIN2)
        r.op("vector", lambda eng, qt=qt: eng.tensor_scalar(
            A_[qt][:, :], A_[qt][:, :], stcol[:, qt:qt + 1], stcol[:, 16 + qt:17 + qt],
            ALU.mult, ALU.add), [(S_["n1"], 1)])

    def emit_x1tx(qt):
        r.wait("tensor", S_["n1"], qt + 1)
        for mc in range(16):
            bank = tx_state[0] % 2
            if ps_t_last[bank] is not None:
                sem, cntv = ps_t_last[bank]
                r.wait("tensor", sem, cntv)
            r.op("tensor", lambda eng, qt=qt, mc=mc, bank=bank:
                 eng.transpose(psT[:, bank * 1024:bank * 1024 + 128],
                               A_[qt][:, mc * 128:(mc + 1) * 128], identb[:, :]),
                 [(S_["tx"], 1)])
            r.wait("vector", S_["tx"], S_["tx"].n)
            r.op("vector", lambda eng, qt=qt, mc=mc, bank=bank:
                 eng.tensor_copy(x1T_block(mc, qt), psT[:, bank * 1024:bank * 1024 + 128]),
                 [(S_["txev"], 1)])
            ps_t_last[bank] = (S_["txev"], S_["txev"].n)
            tx_state[0] += 1

    def emit_h_router(qt):
        hhalf = qt % 2
        r.wait("tensor", S_["txev"], 16 * (qt + 1))
        if qt == 0:
            r.wait("tensor", S_["wF"], 16)
            r.wait("tensor", S_["bc"], 16)
            r.wait("tensor", S_["ld0g"], 16)
        r.wait("tensor", S_["mk"], qt - 1)
        r.wait("tensor", S_["lg"], qt - 1)
        for mc in range(16):
            r.op("tensor", lambda eng, qt=qt, mc=mc, hhalf=hhalf:
                 eng.matmul(psD[:, hhalf * 512:(hhalf + 1) * 512],
                            x1T_block(mc, qt), C_[0][:, mc * 512:(mc + 1) * 512],
                            start=(mc == 0), stop=(mc == 15)), [])
            r.op("tensor", lambda eng, qt=qt, mc=mc, hhalf=hhalf:
                 eng.matmul(psA[:, hhalf * 512:hhalf * 512 + 8],
                            x1T_block(mc, qt), wrtb[:, mc * 8:(mc + 1) * 8],
                            start=(mc == 0), stop=(mc == 15)),
                 [(S_["h"], 1)] if mc == 15 else [])

    def emit_router_chain(qt):
        hhalf = qt % 2
        lg8 = lgts[:, (qt % 2) * 8:(qt % 2) * 8 + 8]
        r.wait("vector", S_["h"], qt + 1)
        if qt == 0:
            r.wait("vector", S_["ld0"], lds0)
        if qt >= 2:
            r.wait("vector", S_["sch"], e8_count_at[qt - 2])
        r.op("vector", lambda eng, qt=qt, hhalf=hhalf, lg8=lg8:
             eng.tensor_add(lg8, psA[:, hhalf * 512:hhalf * 512 + 8], brs[:, :]),
             [(S_["lg"], 1)])
        if qt >= 1:
            r.wait("vector", S_["sch"], e8_count_at[qt - 1])
        r.wait("vector", S_["lg"], S_["lg"].n)
        r.op("vector", lambda eng, lg8=lg8: eng.reduce_max(smf[:, 16:17], lg8, axis=AX.X,
                                                           negate=True), [(S_["vch"], 1)])
        r.op("vector", lambda eng, lg8=lg8: eng.reduce_max(smf[:, 17:18], lg8, axis=AX.X),
             [(S_["vch"], 1)])
        r.wait("scalar", S_["vch"], S_["vch"].n)
        r.op("scalar", lambda eng, lg8=lg8: eng.activation(smf[:, 0:8], lg8, AF.Exp,
                                                           bias=smf[:, 16:17],
                                                           accum_out=smf[:, 18:19]),
             [(S_["sch"], 1)])
        e8_count_at[qt] = S_["sch"].n
        r.wait("vector", S_["sch"], S_["sch"].n)
        r.op("vector", lambda eng: eng.reciprocal(smf[:, 19:20], smf[:, 18:19]),
             [(S_["vch"], 1)])
        r.op("vector", lambda eng, lg8=lg8: eng.tensor_scalar(smf[:, 8:16], lg8,
                                                              smf[:, 17:18], None, ALU.is_ge),
             [(S_["vch"], 1)])
        r.wait("vector", S_["vch"], S_["vch"].n)
        r.op("vector", lambda eng, qt=qt: eng.tensor_scalar(coefT[:, qt * 8:(qt + 1) * 8],
                                                            smf[:, 8:16], smf[:, 19:20],
                                                            None, ALU.mult), [(S_["co"], 1)])

    def emit_mask(qt):
        hhalf = qt % 2
        r.wait("vector", S_["co"], qt + 1)
        if qt >= 2:
            r.wait("vector", S_["txes"], mh_txes_after[qt - 2])
        for ex in range(NE):
            r.op("vector", lambda eng, qt=qt, ex=ex, hhalf=hhalf:
                 eng.tensor_scalar(mh[:, hhalf * 512 + ex * 64: hhalf * 512 + (ex + 1) * 64],
                                   psD[:, hhalf * 512 + ex * 64: hhalf * 512 + (ex + 1) * 64],
                                   coefT[:, qt * 8 + ex:qt * 8 + ex + 1], None, ALU.mult),
                 [(S_["mk"], 1)] if ex == NE - 1 else [])

    def emit_mhtx(qt):
        hhalf = qt % 2
        r.wait("tensor", S_["mk"], qt + 1)
        for blk in range(4):
            bank = tx_state[0] % 2
            if ps_t_last[bank] is not None:
                sem, cntv = ps_t_last[bank]
                r.wait("tensor", sem, cntv)
            r.op("tensor", lambda eng, qt=qt, blk=blk, bank=bank, hhalf=hhalf:
                 eng.transpose(psT[:, bank * 1024:bank * 1024 + 128],
                               mh[:, hhalf * 512 + blk * 128: hhalf * 512 + (blk + 1) * 128],
                               identb[:, :]), [(S_["tx"], 1)])
            if qt == 0 and blk == 0:
                r.wait("scalar", S_["r1"], 48)
            r.wait("scalar", S_["tx"], S_["tx"].n)
            r.op("scalar", lambda eng, qt=qt, blk=blk, bank=bank:
                 eng.copy(C_[2][:, blk * 2048 + qt * 128: blk * 2048 + (qt + 1) * 128],
                          psT[:, bank * 1024:bank * 1024 + 128]), [(S_["txes"], 1)])
            ps_t_last[bank] = (S_["txes"], S_["txes"].n)
            tx_state[0] += 1
        mh_txes_after[qt] = S_["txes"].n

    def emit_y(qt):
        r.wait("tensor", S_["txes"], S_["txes"].n)
        for oc in range(4):
            yidx = qt * 4 + oc
            ybank = yidx % 2
            r.wait("tensor", S_["x2"], yidx - 1)
            for pair in range(4):
                r.op("tensor", lambda eng, qt=qt, oc=oc, pair=pair, ybank=ybank:
                     eng.matmul(psE[:, ybank * 512:(ybank + 1) * 512],
                                C_[2][:, pair * 2048 + qt * 128: pair * 2048 + (qt + 1) * 128],
                                C_[1][:, pair * 2048 + oc * 512: pair * 2048 + (oc + 1) * 512],
                                start=(pair == 0), stop=(pair == 3)),
                     [(S_["yy"], 1)] if pair == 3 else [])
            r.wait("vector", S_["yy"], yidx + 1)
            r.op("vector", lambda eng, qt=qt, oc=oc, ybank=ybank:
                 eng.scalar_tensor_tensor(A_[qt][:, oc * 512:(oc + 1) * 512],
                                          psE[:, ybank * 512:(ybank + 1) * 512],
                                          1.0 / 64.0, A_[qt][:, oc * 512:(oc + 1) * 512],
                                          ALU.mult, ALU.add), [(S_["x2"], 1)])

    def emit_ln2_stats(qt):
        r.wait("vector", S_["x2"], 4 * (qt + 1))
        r.op("vector", lambda eng, qt=qt: eng.reduce_sum(stcol[:, 32 + qt:33 + qt],
                                                         A_[qt][:, :], axis=AX.X),
             [(S_["st1"], 1)])
        r.wait("scalar", S_["x2"], 4 * (qt + 1))
        r.wait("scalar", S_["sq1"], S_["sq1"].n)
        r.op("scalar", lambda eng, qt=qt: eng.activation(X[:, :], A_[qt][:, :],
                                                         AF.Square,
                                                         accum_out=stcol[:, 48 + qt:49 + qt]),
             [(S_["sq1"], 1)])

    def emit_p5(qt, fin):
        r.wait("vector", S_["vch"], fin)
        r.wait("vector", S_["out"], 16 * (qt - 1))
        r.op("vector", lambda eng, qt=qt: eng.tensor_scalar(
            obuf[qt % 2][:, :], A_[qt][:, :], stcol[:, 32 + qt:33 + qt],
            stcol[:, 48 + qt:49 + qt], ALU.mult, ALU.add), [(S_["n2"], 1)])
        r.wait("sync", S_["n2"], qt + 1)
        r.wait("sync", S_["out"], 16 * qt)
        r.op("sync", lambda eng, qt=qt: eng.dma_start(y.ap()[qt * 128:(qt + 1) * 128, :],
                                                      obuf[qt % 2][:, :]), [(S_["out"], 16)])

    for qt in range(16):
        emit_normalize(qt)
    emit_x1tx(0)
    LN2_FIN1 = None
    for qt in range(16):
        emit_h_router(qt)
        emit_router_chain(qt)
        emit_mask(qt)
        if qt + 1 < 16:
            emit_x1tx(qt + 1)
        emit_mhtx(qt)
        emit_y(qt)
        emit_ln2_stats(qt)
        if qt == 7:
            r.wait("vector", S_["st1"], 24)
            r.wait("vector", S_["sq1"], 24)
            ln_finalize(32, 48, 0, 16, 0, 8)
            LN2_FIN1 = S_["vch"].n
        if qt >= 8:
            emit_p5(qt - 8, LN2_FIN1)
    r.wait("vector", S_["st1"], 32)
    r.wait("vector", S_["sq1"], 32)
    ln_finalize(32, 48, 0, 16, 8, 8)
    LN2_FIN2 = S_["vch"].n
    for qt in range(8, 16):
        emit_p5(qt, LN2_FIN2)
    r.wait("sync", S_["out"], 16 * 16)

    r.check()

    with nc.Block() as block:
        @block.tensor
        def _(eng):
            for f in r.ops["tensor"]:
                f(eng)

        @block.scalar
        def _(eng):
            for f in r.ops["scalar"]:
                f(eng)

        @block.vector
        def _(eng):
            for f in r.ops["vector"]:
                f(eng)

        @block.sync
        def _(eng):
            for f in r.ops["sync"]:
                f(eng)

        @block.gpsimd
        def _(eng):
            for f in r.ops["gpsimd"]:
                f(eng)

    ctx.close()
    return nc


_NC_CACHE = None


def _get_nc():
    global _NC_CACHE
    if _NC_CACHE is None:
        _NC_CACHE = build_nc()
    return _NC_CACHE


def make_in_maps(L0_attn_x, L0_attn_wq, L0_attn_wk, L0_attn_wv, L0_attn_wo,
                 L0_ffn_A, L0_ffn_B, L0_ffn_Wr, L0_ffn_br):
    x = np.asarray(L0_attn_x, dtype=np.float32)
    shared = {
        "wqT": np.ascontiguousarray(np.asarray(L0_attn_wq, np.float32).T),
        "wkT": np.ascontiguousarray(np.asarray(L0_attn_wk, np.float32).T),
        "wvT": np.ascontiguousarray(np.asarray(L0_attn_wv, np.float32).T),
        "woT": np.ascontiguousarray(np.asarray(L0_attn_wo, np.float32).T),
        "ATd": np.ascontiguousarray(
            np.asarray(L0_ffn_A, np.float32).transpose(2, 0, 1).reshape(D, NE * RK)),
        "BTd": np.ascontiguousarray(
            np.asarray(L0_ffn_B, np.float32).transpose(0, 2, 1).reshape(4 * 128, D)),
        "WrTd": np.ascontiguousarray(np.asarray(L0_ffn_Wr, np.float32).T),
        "brb": np.ascontiguousarray(
            np.broadcast_to(np.asarray(L0_ffn_br, np.float32).reshape(1, NE), (128, NE))),
        "identd": np.eye(128, dtype=ml_dtypes.bfloat16),
        "onesmd": np.ones((128, 128), dtype=np.float32),
        "onesrd": np.ones((1, 128), dtype=ml_dtypes.bfloat16),
        "onesmbd": np.ones((128, 128), dtype=ml_dtypes.bfloat16),
    }
    in_maps = []
    for c in range(8):
        b, hf = c // 2, c % 2
        own = x[b, hf * NQ:(hf + 1) * NQ, :]
        m = dict(shared)
        m["xTb"] = np.ascontiguousarray(own.T)
        m["xnat"] = np.ascontiguousarray(own)
        in_maps.append(m)
    return in_maps


def kernel(**inputs):
    in_maps = make_in_maps(**inputs)
    nc = _get_nc()
    res = run_bass_kernel_spmd(nc, in_maps, core_ids=list(range(8)))
    out = np.empty((4, 4096, 2048), dtype=np.float32)
    for c in range(8):
        b, hf = c // 2, c % 2
        out[b, hf * NQ:(hf + 1) * NQ, :] = res.results[c]["y"]
    return out
